# version 17
# baseline (speedup 1.0000x reference)
import numpy as np
from contextlib import ExitStack
import concourse.bass as bass
import concourse.mybir as mybir
from concourse.bass_utils import run_bass_kernel_spmd

F32 = mybir.dt.float32
BF16 = mybir.dt.bfloat16
I32 = mybir.dt.int32
AF = mybir.ActivationFunctionType
ALU = mybir.AluOpType
AX = mybir.AxisListType

NCORES = 8
D = 1024
DI = 2816
NTOK = 8192
SEQ = 4096
TA = 512
NT = NTOK // TA
NSEQT = SEQ // TA
DC = D // 128
IC = DI // 128
EPS = 1e-5
QMAX = 127.0
MAGIC = 12582912.0


def build_nc(debug=False):
    nc = bass.Bass("TRN2", target_bir_lowering=False, detect_race_conditions=False)

    xT = nc.dram_tensor("xT", [D, NTOK], F32, kind="ExternalInput").ap()
    W_in = nc.dram_tensor("W_in", [D, D], F32, kind="ExternalInput").ap()
    C_ = nc.dram_tensor("C_", [D, D], F32, kind="ExternalInput").ap()
    b_in = nc.dram_tensor("b_in", [D], F32, kind="ExternalInput").ap()
    log_A = nc.dram_tensor("log_A", [D], F32, kind="ExternalInput").ap()
    B_vec = nc.dram_tensor("B_vec", [D], F32, kind="ExternalInput").ap()
    D_ = nc.dram_tensor("D_", [D], F32, kind="ExternalInput").ap()
    clip_x = nc.dram_tensor("clip_x", [1, 1], F32, kind="ExternalInput").ap()
    clip_down = nc.dram_tensor("clip_down", [1, 1], F32, kind="ExternalInput").ap()
    wg = nc.dram_tensor("wg", [DI, D], F32, kind="ExternalInput").ap()
    wu = nc.dram_tensor("wu", [DI, D], F32, kind="ExternalInput").ap()
    wd = nc.dram_tensor("wd", [D, DI], F32, kind="ExternalInput").ap()
    eye = nc.dram_tensor("eye", [128, 128], F32, kind="ExternalInput").ap()

    out_d = nc.dram_tensor("out", [D, NTOK], F32, kind="ExternalOutput").ap()

    dbg = "ExternalOutput" if debug else "Internal"
    xmid_d = nc.dram_tensor("xmid_d", [D, NTOK], F32, kind=dbg).ap()
    xln2_d = nc.dram_tensor("xln2_d", [D, NTOK], BF16, kind=dbg).ap()
    s_d = nc.dram_tensor("s_d", [DI, NTOK], F32, kind=dbg).ap()
    wgq_d = nc.dram_tensor("wgq_d", [DI, D], BF16, kind=dbg).ap()
    wuq_d = nc.dram_tensor("wuq_d", [DI, D], BF16, kind="Internal").ap()
    wdq_d = nc.dram_tensor("wdq_d", [D, DI], BF16, kind="Internal").ap()
    cc1_in = nc.dram_tensor("cc1_in", [128], F32, kind="Internal").ap()
    cc1_out = nc.dram_tensor("cc1_out", [128], F32, kind="Internal").ap()
    cc2_in = nc.dram_tensor("cc2_in", [128], F32, kind="Internal").ap()
    cc2_out = nc.dram_tensor("cc2_out", [128], F32, kind="Internal").ap()
    gs1_d = nc.dram_tensor("gs1_d", [1, 1], F32, kind="Internal").ap()
    gs2_d = nc.dram_tensor("gs2_d", [1, 1], F32, kind="Internal").ap()

    es = ExitStack()

    ab16 = es.enter_context(nc.sbuf_tensor("ab16", [128, 62592], BF16))
    _b = lambda o, n: ab16[:, o:o + n]
    winb = _b(0, 8192)
    cb = _b(8192, 8192)
    xln2in = _b(0, 8192).rearrange("p (b c t) -> p b c t", b=2, c=DC)
    xqb = _b(8192, 8192).rearrange("p (b c t) -> p b c t", b=2, c=DC)
    dgb = _b(16384, 1024)
    onesb = _b(17408, 128)
    AW = 17536
    xb2 = _b(AW + 0, 8192).rearrange("p (b c t) -> p b c t", b=2, c=DC)
    x2b = _b(AW + 8192, 4096).rearrange("p (c t) -> p c t", c=DC)
    tt = _b(AW + 40960, 4096).rearrange("p (c t) -> p c t", c=DC)
    xlnb2 = _b(AW + 12288, 8192).rearrange("p (b c t) -> p b c t", b=2, c=DC)
    hb = _b(AW + 24576, 4096).rearrange("p (c t) -> p c t", c=DC)
    xmb_sl = [_b(AW + 28672, 4096).rearrange("p (c t) -> p c t", c=DC),
              _b(AW + 20480, 4096).rearrange("p (c t) -> p c t", c=DC)]
    xm2b = _b(AW + 32768, 4096).rearrange("p (c t) -> p c t", c=DC)
    xl2 = _b(AW + 36864, 4096).rearrange("p (c t) -> p c t", c=DC)
    xl2f = _b(AW + 36864, 4096)
    wgqT = _b(AW, 22528).rearrange("p (h n) -> p h n", h=DC)
    wuqT = _b(AW + 22528, 22528).rearrange("p (h n) -> p h n", h=DC)
    wdqT = _b(AW, 22528).rearrange("p (i n) -> p i n", i=IC)
    hqb = _b(AW + 22528, 22528).rearrange("p (b i t) -> p b i t", b=2, i=IC)

    af32 = es.enter_context(nc.sbuf_tensor("af32", [128, 15872], F32))
    _f = lambda o, n: af32[:, o:o + n]
    acol = _f(0, 8)
    bvcol = _f(8, 8)
    bbvcol = _f(16, 8)
    dcol = _f(24, 8)
    rsg = _f(32, 22)
    rsu = _f(54, 22)
    rsd = _f(76, 8)
    escg = _f(84, 22)
    escu = _f(106, 22)
    escd = _f(128, 8)
    invsx = _f(136, 1)
    invs2 = _f(137, 1)
    gmax1 = _f(138, 1)
    gmax2 = _f(139, 1)
    sigx = _f(140, 1)
    sigd = _f(141, 1)
    sxc = _f(142, 1)
    s2c = _f(143, 1)
    clipx_t = _f(144, 1)
    clipd_t = _f(145, 1)
    lmax1 = _f(146, 1)
    lmax2 = _f(147, 1)
    invq = _f(148, 1)
    bcol = _f(152, 8)
    loc = _f(160, 16)
    ecol = _f(176, 8)
    smaxs = _f(192, 352)
    xaring = _f(1024, 4096).rearrange("p (r t) -> p r t", r=8)
    xmring = _f(5120, 2048).rearrange("p (r t) -> p r t", r=4)
    la1 = _f(7168, 512)
    rstd1 = _f(7680, 512)
    mr1 = _f(8192, 512)
    la2 = _f(8704, 512)
    rstd2 = _f(9216, 512)
    mr2 = _f(9728, 512)
    r1t = la1
    r2t = la2
    prepf = _f(10240, 5632).rearrange("p (r n) -> p r n", r=2)
    pf_s = [_f(10240 + 1024 * r, 1024) for r in range(4)]
    pf_A = _f(10240, 2816)
    pf_B = _f(13056, 2816)
    sgring = _f(10240, 1024).rearrange("p (r t) -> p r t", r=2)
    sring = _f(11264, 2048).rearrange("p (r t) -> p r t", r=4)
    sinring = _f(10240, 1536).rearrange("p (r t) -> p r t", r=3)
    xmin = _f(11776, 2048).rearrange("p (r t) -> p r t", r=4)
    outring = _f(13824, 1024).rearrange("p (r t) -> p r t", r=2)
    qfB = _f(14848, 512)
    qfC = _f(15360, 512)
    iv2 = _f(149, 1)
    epsc = _f(150, 1)
    magc = _f(544, 1)
    nmagc = _f(545, 1)
    nhalf = _f(546, 1)
    iv2s = _f(548, 64)

    brow = es.enter_context(nc.sbuf_tensor("brow", [1, 1024], BF16))
    one512 = es.enter_context(nc.sbuf_tensor("one512", [1, 512], BF16))
    row1 = es.enter_context(nc.sbuf_tensor("row1", [1, 160], F32))
    row2 = es.enter_context(nc.sbuf_tensor("row2", [1, 160], F32))

    pb = [es.enter_context(nc.psum_tensor(f"pb{i}", [128, 512], F32))
          for i in range(8)]
    m_ps1, q_ps1, m_ps2, q_ps2 = pb[0], pb[1], pb[2], pb[3]
    u_ps4 = [pb[4], pb[5], pb[6], pb[7]]
    y_ps = [pb[6], pb[7]]
    y_ps4 = [pb[6], pb[7], pb[4], pb[5]]
    g_ps = [pb[0], pb[1], pb[4], pb[5]]
    u2_ps = [pb[2], pb[3], pb[6], pb[7]]
    o_ps = [pb[0], pb[1], pb[2], pb[3]]

    semnames = [
        "s_sq1", "s_st1", "s_msq1", "s_rc1", "s_sr1", "s_mr1",
        "s_t", "s_xln", "s_u", "s_scan", "s_ymm", "s_res",
        "s_sq2", "s_st2", "s_msq2", "s_rc2", "s_sr2", "s_mr2",
        "s_t2", "s_xl2", "s_x2sp", "s_lmx",
        "s_lf1", "s_b1", "s_g1", "s_scl1",
        "s_q", "s_gm", "s_um", "s_sil", "s_stt", "s_sred",
        "s_Bt",
        "s_lf2", "s_b2", "s_g2", "s_scl2",
        "s_qc", "s_qt", "s_om", "s_ov",
        "s_pq", "s_pl5",
        "s_on", "s_wc1", "s_clp", "s_xmc", "s_ex", "s_dg", "s_sg", "s_cc1", "s_cc2", "s_vch", "s_bv1", "s_bv2",
    ]
    for base, n in [("s_axr", 8), ("s_xbr", 2), ("s_xmbr", 2), ("s_xmsp", 4),
                    ("s_blr", 2), ("s_sspr", 4), ("s_clr", 3), ("s_xmi", 4),
                    ("s_ospr", 2), ("s_pwr", 4), ("s_plr", 4), ("s_wtr", 4)]:
        semnames += [f"{base}{r}" for r in range(n)]
    sem = {n: es.enter_context(nc.semaphore(n)) for n in semnames}
    axr = [sem[f"s_axr{r}"] for r in range(8)]
    xbr = [sem[f"s_xbr{r}"] for r in range(2)]
    xmbr = [sem[f"s_xmbr{r}"] for r in range(2)]
    xmspr = [sem[f"s_xmsp{r}"] for r in range(4)]
    blr = [sem[f"s_blr{r}"] for r in range(2)]
    sspr = [sem[f"s_sspr{r}"] for r in range(4)]
    clr = [sem[f"s_clr{r}"] for r in range(3)]
    xmir = [sem[f"s_xmi{r}"] for r in range(4)]
    ospr = [sem[f"s_ospr{r}"] for r in range(2)]
    pwr = [sem[f"s_pwr{r}"] for r in range(4)]
    plr = [sem[f"s_plr{r}"] for r in range(4)]
    wtr = [sem[f"s_wtr{r}"] for r in range(4)]

    cols = lambda k: slice(k * TA, (k + 1) * TA)
    val = lambda t: 0 <= t < NT

    prep_tiles = ([("c", i) for i in range(DC)] + [("g", i) for i in range(IC)]
                  + [("u", i) for i in range(IC)] + [("d", i) for i in range(DC)])
    NSMALL = DC + 2 * IC

    def prep_ring(n):
        if n < NSMALL:
            return n % 4
        return 0 if (n - NSMALL) % 2 == 0 else 2

    def pfv(n):
        if n < NSMALL:
            return pf_s[n % 4]
        return pf_A if (n - NSMALL) % 2 == 0 else pf_B

    _plr_cnt, _pwr_cnt = {}, {}
    _ring_loads, _ring_stores = [0] * 4, [0] * 4
    _last_user = {0: None, 1: None, 2: None, 3: None}
    prep_load_guards = {}
    for _n, (_w, _i) in enumerate(prep_tiles):
        _r = prep_ring(_n)
        if _n < NSMALL:
            _ov = [_n % 4]
        elif _r == 0:
            _ov = [0, 1, 2]
        else:
            _ov = [2, 3]
        _qmax, _stg = None, {}
        for _sl in _ov:
            _p = _last_user[_sl]
            if _p is not None:
                _qmax = _p if _qmax is None else max(_qmax, _p)
                if prep_tiles[_p][0] != "c":
                    _pr = prep_ring(_p)
                    _stg[_pr] = max(_stg.get(_pr, 0), _pwr_cnt[_p])
        prep_load_guards[_n] = (_qmax, sorted(_stg.items()))
        _ring_loads[_r] += 1
        _plr_cnt[_n] = _ring_loads[_r]
        if _w != "c":
            _ring_stores[_r] += 1
            _pwr_cnt[_n] = _ring_stores[_r]
        for _sl in _ov:
            _last_user[_sl] = _n
    PSTORE_RING = list(_ring_stores)

    def pwr_counts_through(m):
        c = [0] * 4
        for _n2 in range(m + 1):
            if prep_tiles[_n2][0] != "c":
                c[prep_ring(_n2)] += 1
        return c
    PREP_PER_IT = 4
    NPREP = len(prep_tiles)
    def prep_chunk(it):
        if it <= 12:
            return range(PREP_PER_IT * it, min(NSMALL, PREP_PER_IT * (it + 1)))
        n = NSMALL + (it - 13)
        return range(n, n + 1) if n < NPREP else range(0, 0)
    PREP_LAST_IT = 13 + (NPREP - NSMALL) - 1
    prep_src = {"g": wg, "u": wu, "d": wd, "c": C_}
    prep_dst = {"g": wgq_d, "u": wuq_d, "d": wdq_d}
    prep_rs = {"g": rsg, "u": rsu, "d": rsd}
    prep_w = {"g": 1024, "u": 1024, "d": 2816, "c": 1024}

    vch_after = {}
    _cnt = 0
    for _n, (_w, _i) in enumerate(prep_tiles):
        if _w != "c":
            _cnt += 3
        vch_after[_n] = _cnt

    def prev_bank_user(gidx):
        ic = gidx % IC
        jt = gidx // IC
        if ic >= 4:
            return gidx - 4
        if jt == 0:
            return -1
        return IC * (jt - 1) + [20, 21, 18, 19][ic]

    with es, nc.Block() as block:

        @block.sync
        def _(sync):
            sync.dma_start(out=acol[:, :], in_=log_A.rearrange("(c p) -> p c", p=128), allow_slow_non_contiguous=True).then_inc(sem["s_pl5"], 16)
            sync.dma_start(out=bvcol[:, :], in_=B_vec.rearrange("(c p) -> p c", p=128), allow_slow_non_contiguous=True).then_inc(sem["s_pl5"], 16)
            sync.dma_start(out=bcol[:, :], in_=b_in.rearrange("(c p) -> p c", p=128), allow_slow_non_contiguous=True).then_inc(sem["s_pl5"], 16)
            sync.dma_start(out=dcol[:, :], in_=D_.rearrange("(c p) -> p c", p=128), allow_slow_non_contiguous=True).then_inc(sem["s_pl5"], 16)
            sync.dma_start(out=pf_s[0][:, :128], in_=eye).then_inc(sem["s_pl5"], 16)

            def _prep_load_sync(n):
                w, i = prep_tiles[n]
                qmax, stg = prep_load_guards[n]
                if qmax is not None:
                    sync.wait_ge(sem["s_pq"], qmax + 1)
                for rr, cc in stg:
                    sync.wait_ge(pwr[rr], 16 * cc)
                sync.dma_start(out=pfv(n)[:, :],
                               in_=prep_src[w][bass.ts(i, 128), :]).then_inc(plr[prep_ring(n)], 16)

            for it in range(NT + 3):
                if it == 17:
                    _cg = pwr_counts_through(29)
                    for rr in range(4):
                        if _cg[rr]:
                            sync.wait_ge(pwr[rr], 16 * _cg[rr])
                    sync.wait_ge(sem["s_st1"], NT)
                    sync.wait_ge(sem["s_t"], NT)
                    for hc in range(4):
                        sync.dma_start_transpose(wgqT[:, hc, :], wgq_d[:, bass.ts(hc, 128)]).then_inc(wtr[hc % 4], 16)
                if it == 18:
                    sync.wait_ge(sem["s_u"], 8 * NT)
                    sync.wait_ge(sem["s_ymm"], 8 * NT)
                    for hc in range(4, 7):
                        sync.dma_start_transpose(wgqT[:, hc, :], wgq_d[:, bass.ts(hc, 128)]).then_inc(wtr[hc % 4], 16)
                k2 = it - 2
                if val(k2):
                    for dc in range(DC):
                        gdc = 8 * k2 + dc
                        sync.wait_ge(sem["s_res"], gdc + 1)
                        sync.dma_start(out=xmid_d[bass.ts(dc, 128), cols(k2)],
                                       in_=xmring[:, gdc % 4, :]).then_inc(xmspr[gdc % 4], 16)
                k_ld = it - 1
                if val(k_ld):
                    for dc in range(DC):
                        gdc = 8 * k_ld + dc
                        sync.wait_ge(sem["s_res"], max(0, gdc - 7))
                        sync.dma_start(out=xaring[:, gdc % 8, :],
                                       in_=xT[bass.ts(dc, 128), cols(k_ld)]).then_inc(axr[gdc % 8], 16)
                k_sp = it - 3
                if val(k_sp):
                    sync.wait_ge(sem["s_xl2"], k_sp + 1)
                    sync.dma_start(
                        out=xln2_d[:, cols(k_sp)].rearrange("(c p) t -> p c t", p=128),
                        in_=xl2[:, :, :]).then_inc(sem["s_x2sp"], 16)
                for n in prep_chunk(it):
                    if n >= NSMALL:
                        _prep_load_sync(n)

            for it2 in range(NT + 3, PREP_LAST_IT + 1):
                for n in prep_chunk(it2):
                    if n >= NSMALL:
                        _prep_load_sync(n)

            sync.wait_ge(sem["s_ymm"], 8 * NT)
            for jt in range(2):
                sync.wait_ge(sem["s_x2sp"], 16 * (jt + 1))
                sync.dma_start(out=xln2in[:, jt % 2, :, :],
                               in_=xln2_d[:, cols(jt)].rearrange("(c p) t -> p c t", p=128)).then_inc(blr[jt % 2], 16)
            sync.wait_ge(sem["s_x2sp"], 16 * NT)
            sync.wait_ge(sem["s_res"], 8 * NT)
            sync.wait_ge(sem["s_lmx"], NT)
            _cgu = pwr_counts_through(NSMALL - 1)
            for rr in range(4):
                if _cgu[rr]:
                    sync.wait_ge(pwr[rr], 16 * _cgu[rr])
            for hc in range(7, DC):
                j = hc
                sync.wait_ge(wtr[j % 4], 16 * (j // 4))
                sync.dma_start_transpose(wgqT[:, hc, :], wgq_d[:, bass.ts(hc, 128)]).then_inc(wtr[j % 4], 16)
            for hc in range(DC):
                j = 8 + hc
                sync.wait_ge(wtr[j % 4], 16 * (j // 4))
                sync.dma_start_transpose(wuqT[:, hc, :], wuq_d[:, bass.ts(hc, 128)]).then_inc(wtr[j % 4], 16)

            for jt in range(2, NT):
                sync.wait_ge(sem["s_q"], max(0, jt - 1))
                sync.dma_start(out=xln2in[:, jt % 2, :, :],
                               in_=xln2_d[:, cols(jt)].rearrange("(c p) t -> p c t", p=128)).then_inc(blr[jt % 2], 16)

            sync.wait_ge(sem["s_Bt"], NT)
            for rr in range(4):
                if PSTORE_RING[rr]:
                    sync.wait_ge(pwr[rr], 16 * PSTORE_RING[rr])
            for ic in range(IC):
                j = 16 + ic
                sync.wait_ge(wtr[j % 4], 16 * (j // 4))
                sync.dma_start_transpose(wdqT[:, ic, :], wdq_d[:, bass.ts(ic, 128)]).then_inc(wtr[j % 4], 16)

            for mt in range(NT):
                for ic in range(IC):
                    gic = IC * mt + ic
                    sync.wait_ge(sem["s_qc"], max(0, gic - 2))
                    sync.wait_ge(sspr[gic % 4], 16 * (gic // 4 + 1))
                    sync.dma_start(out=sinring[:, gic % 3, :],
                                   in_=s_d[bass.ts(ic, 128), cols(mt)]).then_inc(clr[gic % 3], 16)

        @block.gpsimd
        def _(g):
            g.memset(onesb[:, :], 1.0 / 1024.0).then_inc(sem["s_on"], 1)
            g.memset(one512[:1, :], 1.0).then_inc(sem["s_on"], 1)
            g.memset(magc[:, :], MAGIC).then_inc(sem["s_on"], 1)
            g.memset(nmagc[:, :], -MAGIC).then_inc(sem["s_on"], 1)
            g.memset(nhalf[:, :], -0.5).then_inc(sem["s_on"], 1)
            g.dma_start(out=brow[:1, :], in_=b_in.rearrange("(o d) -> o d", o=1)).then_inc(sem["s_clp"], 16)
            g.dma_start(out=clipx_t[:, :], in_=clip_x.broadcast_to([128, 1])).then_inc(sem["s_clp"], 16)
            g.dma_start(out=clipd_t[:, :], in_=clip_down.broadcast_to([128, 1])).then_inc(sem["s_clp"], 16)

            for it in range(NT + 3):
                k0 = it
                if val(k0):
                    g.wait_ge(sem["s_sq1"], max(0, k0 - 1))
                    g.wait_ge(sem["s_st1"], max(0, k0 - 1))
                    g.dma_start(out=xb2[:, k0 % 2, :, :],
                                in_=xT[:, cols(k0)].rearrange("(c p) t -> p c t", p=128)).then_inc(xbr[k0 % 2], 16)
                if it < 2:
                    for dc in range(4 * it, 4 * it + 4):
                        g.dma_start(out=winb[:, bass.ts(dc, 1024)],
                                    in_=W_in[bass.ts(dc, 128), :]).then_inc(sem["s_wc1"], 16)
                k3 = it - 3
                if val(k3):
                    g.wait_ge(sem["s_xmc"], 8 * (k3 + 1))
                    g.wait_ge(sem["s_sr2"], k3 + 1)
                    g.wait_ge(sem["s_st2"], k3 + 1)
                    g.wait_ge(sem["s_x2sp"], 16 * k3)
                    g.wait_ge(sem["s_lmx"], k3)
                    rb2 = rstd2.rearrange("p (o t) -> p o t", o=1).broadcast_to([128, DC, TA])
                    g.tensor_tensor(xl2[:, :, :], xmb_sl[k3 % 2][:, :, :], rb2, ALU.mult).then_inc(sem["s_t2"], 1)
                k1g = it - 1
                if val(k1g):
                    g.wait_ge(sem["s_sr1"], k1g + 1)
                    g.wait_ge(xbr[k1g % 2], 16 * (k1g // 2 + 1))
                    rb1 = rstd1.rearrange("p (o t) -> p o t", o=1).broadcast_to([128, DC, TA])
                    g.tensor_tensor(tt[:, :, :], xb2[:, k1g % 2, :, :], rb1,
                                    ALU.mult).then_inc(sem["s_t"], 1)
                    g.wait_ge(sem["s_mr1"], k1g + 1)
                    g.wait_ge(sem["s_u"], 8 * max(0, k1g - 1))
                    g.wait_ge(sem["s_ymm"], 8 * max(0, k1g - 1))
                    mb1 = mr1.rearrange("p (o t) -> p o t", o=1).broadcast_to([128, DC, TA])
                    g.tensor_tensor(xlnb2[:, k1g % 2, :, :], tt[:, :, :], mb1,
                                    ALU.subtract).then_inc(sem["s_xln"], 1)
                for n in prep_chunk(it):
                    w, i = prep_tiles[n]
                    if w == "c":
                        continue
                    g.wait_ge(sem["s_pq"], n + 1)
                    g.wait_ge(pwr[prep_ring(n)], 16 * (_pwr_cnt[n] - 1))
                    g.dma_start(out=prep_dst[w][bass.ts(i, 128), :],
                                in_=pfv(n)[:, :]).then_inc(pwr[prep_ring(n)], 16)

            for it2 in range(NT + 3, PREP_LAST_IT + 1):
                for n in prep_chunk(it2):
                    w, i = prep_tiles[n]
                    g.wait_ge(sem["s_pq"], n + 1)
                    g.wait_ge(pwr[prep_ring(n)], 16 * (_pwr_cnt[n] - 1))
                    g.dma_start(out=prep_dst[w][bass.ts(i, 128), :],
                                in_=pfv(n)[:, :]).then_inc(pwr[prep_ring(n)], 16)

            g.wait_ge(sem["s_lf1"], 1)
            g.dma_start(out=cc1_in, in_=lmax1[:, :]).then_inc(sem["s_b1"], 16)
            g.wait_ge(sem["s_b1"], 16)
            g.collective_compute("AllReduce", ALU.max,
                                 replica_groups=[list(range(NCORES))],
                                 ins=[cc1_in], outs=[cc1_out]).then_inc(sem["s_cc1"], 1)
            g.wait_ge(sem["s_cc1"], 1)
            g.dma_start(out=row1[:1, :128], in_=cc1_out).then_inc(sem["s_b1"], 16)
            g.wait_ge(sem["s_g1"], 1)
            g.dma_start(out=gs1_d, in_=row1[:1, 128:129]).then_inc(sem["s_b1"], 16)
            g.wait_ge(sem["s_b1"], 48)
            g.dma_start(out=gmax1[:, :], in_=gs1_d.broadcast_to([128, 1])).then_inc(sem["s_b1"], 16)

            g.wait_ge(sem["s_lf2"], 1)
            g.dma_start(out=cc2_in, in_=lmax2[:, :]).then_inc(sem["s_b2"], 16)
            g.wait_ge(sem["s_b2"], 16)
            g.collective_compute("AllReduce", ALU.max,
                                 replica_groups=[list(range(NCORES))],
                                 ins=[cc2_in], outs=[cc2_out]).then_inc(sem["s_cc2"], 1)
            g.wait_ge(sem["s_cc2"], 1)
            g.dma_start(out=row2[:1, :128], in_=cc2_out).then_inc(sem["s_b2"], 16)
            g.wait_ge(sem["s_g2"], 1)
            g.dma_start(out=gs2_d, in_=row2[:1, 128:129]).then_inc(sem["s_b2"], 16)
            g.wait_ge(sem["s_b2"], 48)
            g.dma_start(out=gmax2[:, :], in_=gs2_d.broadcast_to([128, 1])).then_inc(sem["s_b2"], 16)

            for mt in range(NT):
                for hc in range(DC):
                    ghc = 8 * mt + hc
                    g.wait_ge(sem["s_ov"], max(0, ghc - 7))
                    g.dma_start(out=xaring[:, ghc % 8, :],
                                in_=xmid_d[bass.ts(hc, 128), cols(mt)]).then_inc(axr[ghc % 8], 16)
            g.wait_ge(ospr[0], 16 * 8 * NT // 2)
            g.wait_ge(ospr[1], 16 * 8 * NT // 2)

        @block.tensor
        def _(t):
            t.wait_ge(sem["s_on"], 1)
            first_u = [True]
            first_y = [True]
            for it in range(NT + 3):
                k3 = it - 3
                if val(k3):
                    t.wait_ge(sem["s_sq2"], k3 + 1)
                    t.wait_ge(sem["s_xmc"], 8 * (k3 + 1))
                    t.wait_ge(sem["s_mr2"], k3)
                    for c in range(DC):
                        t.matmul(m_ps2[:, :], onesb[:, :], xmb_sl[k3 % 2][:, c, :],
                                 start=(c == 0), stop=(c == DC - 1))
                    for c in range(DC):
                        mm = t.matmul(q_ps2[:, :], onesb[:, :], xm2b[:, c, :],
                                      start=(c == 0), stop=(c == DC - 1))
                    mm.then_inc(sem["s_st2"], 1)
                k1 = it - 1
                if val(k1):
                    t.wait_ge(sem["s_sq1"], k1 + 1)
                    t.wait_ge(xbr[k1 % 2], 16 * (k1 // 2 + 1))
                    t.wait_ge(sem["s_mr1"], k1)
                    for c in range(DC):
                        t.matmul(m_ps1[:, :], onesb[:, :], xb2[:, k1 % 2, c, :],
                                 start=(c == 0), stop=(c == DC - 1))
                    for c in range(DC):
                        mm = t.matmul(q_ps1[:, :], onesb[:, :], x2b[:, c, :],
                                      start=(c == 0), stop=(c == DC - 1))
                    mm.then_inc(sem["s_st1"], 1)
                k2 = it - 2
                if val(k2):
                    if first_u[0]:
                        t.wait_ge(sem["s_wc1"], 16 * 8)
                        t.wait_ge(sem["s_clp"], 48)
                        t.wait_ge(sem["s_on"], 2)
                        first_u[0] = False
                    t.wait_ge(sem["s_xln"], k2 + 1)
                    for ec in range(DC):
                        t.wait_ge(sem["s_scan"], max(0, 8 * k2 + ec - 3))
                        if ec < 4:
                            t.wait_ge(sem["s_res"], max(0, 8 * k2 + (-1, 0, -3, -2)[ec]))
                        for dc in range(DC):
                            t.matmul(u_ps4[ec % 4][:, :],
                                     winb[:, dc * 1024 + ec * 128: dc * 1024 + (ec + 1) * 128],
                                     xlnb2[:, k2 % 2, dc, :],
                                     start=(dc == 0), stop=False)
                        mm = t.matmul(u_ps4[ec % 4][:, :],
                                      brow[:1, bass.ts(ec, 128)], one512[:1, :],
                                      start=False, stop=True)
                        mm.then_inc(sem["s_u"], 1)
                if val(k2):
                    if first_y[0]:
                        t.wait_ge(sem["s_pq"], 8)
                        t.wait_ge(sem["s_dg"], 1)
                        first_y[0] = False
                    for dc in range(DC):
                        t.wait_ge(sem["s_res"], max(0, 8 * k2 + dc - 3))
                        if dc == 0:
                            t.wait_ge(sem["s_scan"], 8 * k2 + 8)
                        t.matmul(y_ps4[dc % 4][:, :], dgb[:, bass.ts(dc, 128)],
                                 xlnb2[:, k2 % 2, dc, :], start=True, stop=False)
                        for sc in range(DC):
                            mm = t.matmul(y_ps4[dc % 4][:, :],
                                          cb[:, sc * 1024 + dc * 128: sc * 1024 + (dc + 1) * 128],
                                          hb[:, sc, :], start=False, stop=(sc == DC - 1))
                        mm.then_inc(sem["s_ymm"], 1)

            t.wait_ge(sem["s_mr1"], NT)
            t.wait_ge(sem["s_mr2"], NT)
            t.wait_ge(sem["s_scan"], 8 * NT)
            t.wait_ge(sem["s_res"], 8 * NT)
            for jt in range(NT):
                t.wait_ge(sem["s_q"], jt + 1)
                for ic in range(IC):
                    gidx = IC * jt + ic
                    t.wait_ge(sem["s_sil"], max(0, gidx - 1))
                    for hc in range(DC):
                        if jt == 0 and ic == 0:
                            t.wait_ge(wtr[hc % 4], 16 * (hc // 4 + 1))
                        mm = t.matmul(g_ps[ic % 2][:, :],
                                      wgqT[:, hc, bass.ts(ic, 128)],
                                      xqb[:, jt % 2, hc, :],
                                      start=(hc == 0), stop=(hc == DC - 1))
                    mm.then_inc(sem["s_gm"], 1)
                    t.wait_ge(sem["s_stt"], max(0, gidx - 1))
                    for hc in range(DC):
                        if jt == 0 and ic == 0:
                            t.wait_ge(wtr[hc % 4], 16 * (hc // 4 + 3))
                        mm = t.matmul(u2_ps[ic % 2][:, :],
                                      wuqT[:, hc, bass.ts(ic, 128)],
                                      xqb[:, jt % 2, hc, :],
                                      start=(hc == 0), stop=(hc == DC - 1))
                    mm.then_inc(sem["s_um"], 1)
                t.nop().then_inc(sem["s_Bt"], 1)

            t.wait_ge(sem["s_sil"], IC * NT)
            t.wait_ge(sem["s_stt"], IC * NT)
            for mt in range(NT):
                for hc in range(DC):
                    ghc = 8 * mt + hc
                    t.wait_ge(sem["s_ov"], max(0, ghc - 3))
                    for ic in range(IC):
                        if hc == 0:
                            t.wait_ge(sem["s_qc"], IC * mt + ic + 1)
                        if mt == 0 and hc == 0:
                            t.wait_ge(wtr[ic % 4], 16 * (ic // 4 + 5))
                        mm = t.matmul(o_ps[hc % 4][:, :],
                                      wdqT[:, ic, bass.ts(hc, 128)],
                                      hqb[:, mt % 2, ic, :],
                                      start=(ic == 0), stop=(ic == IC - 1))
                    mm.then_inc(sem["s_om"], 1)

        @block.scalar
        def _(s):
            s.wait_ge(xbr[0], 16)
            s.activation(x2b[:, :, :], xb2[:, 0, :, :], AF.Square).then_inc(sem["s_sq1"], 1)

            s.wait_ge(sem["s_pl5"], 80)
            s.wait_ge(sem["s_on"], 5)
            s.activation(ecol[:, :], acol[:, :], AF.Exp).then_inc(sem["s_ex"], 1)
            s.wait_ge(sem["s_clp"], 48)
            s.activation(sigx[:, :], clipx_t[:, :], AF.Sigmoid)
            s.activation(sigd[:, :], clipd_t[:, :], AF.Sigmoid).then_inc(sem["s_sg"], 1)

            def _prep_quant_s(n):
                w, i = prep_tiles[n]
                if w == "c":
                    return
                s.wait_ge(sem["s_on"], 4)
                s.wait_ge(sem["s_vch"], vch_after[n])
                s.activation(pfv(n)[:, :], pfv(n)[:, :],
                             AF.Identity, scale=iv2s[:, n:n + 1], bias=magc[:, :1])
                s.activation(pfv(n)[:, :], pfv(n)[:, :],
                             AF.Identity, bias=nmagc[:, :1]).then_inc(sem["s_pq"], 1)

            def _prep_load_s(n):
                w, i = prep_tiles[n]
                if n == 0:
                    s.wait_ge(sem["s_dg"], 1)
                qmax, stg = prep_load_guards[n]
                if qmax is not None:
                    s.wait_ge(sem["s_pq"], qmax + 1)
                for rr, cc in stg:
                    s.wait_ge(pwr[rr], 16 * cc)
                s.dma_start(out=pfv(n)[:, :],
                            in_=prep_src[w][bass.ts(i, 128), :]).then_inc(plr[prep_ring(n)], 16)

            for it in range(NT + 3):
                k0, k1, k2, k3 = it, it - 1, it - 2, it - 3
                for n in prep_chunk(it):
                    if n < NSMALL:
                        _prep_load_s(n)
                if val(k1):
                    s.wait_ge(sem["s_st1"], k1 + 1)
                    s.activation(la1[:, :], m_ps1[:, :], AF.Square).then_inc(sem["s_msq1"], 1)
                if val(k3):
                    s.wait_ge(sem["s_st2"], k3 + 1)
                    s.activation(la2[:, :], m_ps2[:, :], AF.Square).then_inc(sem["s_msq2"], 1)
                if val(k1):
                    s.wait_ge(sem["s_rc1"], k1 + 1)
                    s.wait_ge(sem["s_t"], k1)
                    s.wait_ge(sem["s_mr1"], k1)
                    s.activation(rstd1[:, :], la1[:, :], AF.Ln)
                    s.activation(rstd1[:, :], rstd1[:, :], AF.Exp,
                                 scale=nhalf[:, :1]).then_inc(sem["s_sr1"], 1)
                if val(k3):
                    s.wait_ge(sem["s_rc2"], k3 + 1)
                    s.wait_ge(sem["s_t2"], k3)
                    s.wait_ge(sem["s_mr2"], k3)
                    s.activation(rstd2[:, :], la2[:, :], AF.Ln)
                    s.activation(rstd2[:, :], rstd2[:, :], AF.Exp,
                                 scale=nhalf[:, :1]).then_inc(sem["s_sr2"], 1)
                if val(k0) and k0 > 0:
                    s.wait_ge(xbr[k0 % 2], 16 * (k0 // 2 + 1))
                    s.wait_ge(sem["s_st1"], k0)
                    s.activation(x2b[:, :, :], xb2[:, k0 % 2, :, :], AF.Square).then_inc(sem["s_sq1"], 1)
                if val(k2):
                    s.wait_ge(sem["s_st2"], k2)
                    for dc in range(DC):
                        s.wait_ge(sem["s_xmc"], 8 * k2 + dc + 1)
                        a = s.activation(xm2b[:, dc, :], xmb_sl[k2 % 2][:, dc, :], AF.Square)
                        if dc == DC - 1:
                            a.then_inc(sem["s_sq2"], 1)
                for n in prep_chunk(it):
                    _prep_quant_s(n)

            for it2 in range(NT + 3, PREP_LAST_IT + 1):
                for n in prep_chunk(it2):
                    _prep_quant_s(n)

            s.wait_ge(sem["s_scl1"], 1)
            def _spill_s(g2):
                jt2, ic2 = divmod(g2, IC)
                s.wait_ge(sem["s_sred"], g2 + 1)
                s.dma_start(out=s_d[bass.ts(ic2, 128), cols(jt2)],
                            in_=sring[:, g2 % 4, :]).then_inc(sspr[g2 % 4], 16)
            for jt in range(NT):
                for ic in range(IC):
                    gidx = IC * jt + ic
                    s.wait_ge(sem["s_gm"], gidx + 1)
                    s.wait_ge(sem["s_stt"], max(0, gidx - 1))
                    s.activation(sgring[:, ic % 2, :], g_ps[ic % 2][:, :], AF.Silu,
                                 scale=escg[:, ic:ic + 1]).then_inc(sem["s_sil"], 1)
                    if gidx >= 4:
                        _spill_s(gidx - 4)
            for g2 in range(IC * NT - 4, IC * NT):
                _spill_s(g2)

            for mt in range(NT):
                for hc in range(DC):
                    ghc = 8 * mt + hc
                    s.wait_ge(sem["s_ov"], ghc + 1)
                    s.dma_start(out=out_d[bass.ts(hc, 128), cols(mt)],
                                in_=outring[:, ghc % 2, :]).then_inc(ospr[ghc % 2], 16)
            s.wait_ge(ospr[0], 16 * 8 * NT // 2)
            s.wait_ge(ospr[1], 16 * 8 * NT // 2)

        @block.vector
        def _(v):
            v.wait_ge(sem["s_ex"], 1)
            v.wait_ge(sem["s_pl5"], 80)
            v.tensor_scalar(acol[:, :], ecol[:, :], -1.0, None, ALU.mult)
            v.tensor_tensor(bbvcol[:, :], bcol[:, :], bvcol[:, :], ALU.mult)
            for dc in range(DC):
                v.tensor_scalar(dgb[:, bass.ts(dc, 128)], pf_s[0][:, :128],
                                dcol[:, dc:dc + 1], None, ALU.mult)
            v.engine_nop().then_inc(sem["s_dg"], 1)
            vch = [0]
            def _prep_chain(n):
                w, i = prep_tiles[n]
                v.wait_ge(plr[prep_ring(n)], 16 * _plr_cnt[n])
                if w == "c":
                    v.tensor_scalar(cb[:, bass.ts(i, 1024)], pfv(n)[:, :],
                                    bvcol[:, i:i + 1], None, ALU.mult).then_inc(sem["s_pq"], 1)
                    return
                v.reduce_max(prep_rs[w][:, i:i + 1], pfv(n)[:, :], AX.X,
                             apply_absolute_value=True).then_inc(sem["s_vch"], 1)
                vch[0] += 1
                v.wait_ge(sem["s_vch"], vch[0])
                v.reciprocal(invq[:, :], prep_rs[w][:, i:i + 1]).then_inc(sem["s_vch"], 1)
                vch[0] += 1
                v.wait_ge(sem["s_vch"], vch[0])
                v.tensor_scalar(iv2s[:, n:n + 1], invq[:, :], QMAX, None, ALU.mult).then_inc(sem["s_vch"], 1)
                vch[0] += 1

            for it in range(NT + 3):
                k0, k1, k2, k3 = it, it - 1, it - 2, it - 3
                if val(k1):
                    v.wait_ge(sem["s_msq1"], k1 + 1)
                    v.scalar_tensor_tensor(la1[:, :], q_ps1[:, :], EPS, la1[:, :],
                                           ALU.add, ALU.subtract).then_inc(sem["s_rc1"], 1)
                if val(k3):
                    v.wait_ge(sem["s_msq2"], k3 + 1)
                    v.scalar_tensor_tensor(la2[:, :], q_ps2[:, :], EPS, la2[:, :],
                                           ALU.add, ALU.subtract).then_inc(sem["s_rc2"], 1)
                if val(k1):
                    v.wait_ge(sem["s_sr1"], k1 + 1)
                    v.tensor_tensor(mr1[:, :], m_ps1[:, :], rstd1[:, :], ALU.mult).then_inc(sem["s_mr1"], 1)
                if val(k2):
                    for ec in range(DC):
                        v.wait_ge(sem["s_u"], 8 * k2 + ec + 1)
                        init = 0.0 if (k2 % NSEQT == 0) else hb[:, ec, TA - 1:TA]
                        v.tensor_tensor_scan(hb[:, ec, :],
                                             acol[:, ec:ec + 1].broadcast_to([128, TA]),
                                             u_ps4[ec % 4][:, :], init,
                                             ALU.mult, ALU.add).then_inc(sem["s_scan"], 1)
                if val(k3):
                    v.wait_ge(sem["s_sr2"], k3 + 1)
                    v.tensor_tensor(mr2[:, :], m_ps2[:, :], rstd2[:, :], ALU.mult).then_inc(sem["s_mr2"], 1)

                def _xl2_sub():
                    v.wait_ge(sem["s_t2"], k3 + 1)
                    mb2 = mr2.rearrange("p (o t) -> p o t", o=1).broadcast_to([128, DC, TA])
                    v.tensor_tensor(xl2[:, :, :], xl2[:, :, :], mb2, ALU.subtract).then_inc(sem["s_xl2"], 1)

                def _xl2_red():
                    v.reduce_max(loc[:, k3:k3 + 1], xl2f[:, :], AX.X,
                                 apply_absolute_value=True).then_inc(sem["s_lmx"], 1)

                if val(k2):
                    v.wait_ge(sem["s_sq2"], max(0, k2 - 1))
                    v.wait_ge(sem["s_st2"], max(0, k2 - 1))
                    v.wait_ge(sem["s_t2"], max(0, k2 - 1))
                    for dc in range(DC):
                        gdc = 8 * k2 + dc
                        v.wait_ge(sem["s_ymm"], gdc + 1)
                        v.wait_ge(axr[gdc % 8], 16 * (gdc // 8 + 1))
                        v.wait_ge(xmspr[gdc % 4], 16 * (gdc // 4))
                        v.tensor_tensor(xmring[:, gdc % 4, :], y_ps4[dc % 4][:, :],
                                        xaring[:, gdc % 8, :], ALU.add).then_inc(sem["s_res"], 1)
                        v.tensor_copy(xmb_sl[k2 % 2][:, dc, :],
                                      xmring[:, gdc % 4, :]).then_inc(sem["s_xmc"], 1)
                        if val(k3) and dc == 1:
                            _xl2_sub()
                        if val(k3) and dc == 4:
                            _xl2_red()
                elif val(k3):
                    _xl2_sub()
                    _xl2_red()
                for n in prep_chunk(it):
                    _prep_chain(n)

            for it2 in range(NT + 3, PREP_LAST_IT + 1):
                for n in prep_chunk(it2):
                    _prep_chain(n)

            v.wait_ge(sem["s_lmx"], NT)
            v.reduce_max(lmax1[:, :], loc[:, :NT], AX.X).then_inc(sem["s_lf1"], 1)
            v.wait_ge(sem["s_b1"], 32)
            v.reduce_max(row1[:1, 128:129], row1[:1, :128], AX.X).then_inc(sem["s_g1"], 1)
            v.wait_ge(sem["s_b1"], 64)
            v.wait_ge(sem["s_sg"], 1)
            v.tensor_tensor(sxc[:, :], gmax1[:, :], sigx[:, :], ALU.mult).then_inc(sem["s_bv1"], 1)
            v.wait_ge(sem["s_bv1"], 1)
            v.tensor_scalar(sxc[:, :], sxc[:, :], 1.0 / QMAX, None, ALU.mult).then_inc(sem["s_bv1"], 1)
            v.wait_ge(sem["s_bv1"], 2)
            v.reciprocal(invsx[:, :], sxc[:, :])
            v.tensor_scalar(escg[:, :], rsg[:, :], sxc[:, :], 1.0 / QMAX, ALU.mult, ALU.mult)
            v.tensor_scalar(escu[:, :], rsu[:, :], sxc[:, :], 1.0 / QMAX, ALU.mult, ALU.mult)
            v.engine_nop().then_inc(sem["s_scl1"], 1)

            def _stt(jt2, ic):
                gidx = IC * jt2 + ic
                v.wait_ge(sem["s_um"], gidx + 1)
                v.wait_ge(sem["s_sil"], gidx + 1)
                v.wait_ge(sspr[gidx % 4], 16 * (gidx // 4))
                v.scalar_tensor_tensor(sring[:, gidx % 4, :], u2_ps[ic % 2][:, :],
                                       escu[:, ic:ic + 1], sgring[:, ic % 2, :],
                                       ALU.mult, ALU.mult).then_inc(sem["s_stt"], 1)
                v.reduce_max(smaxs[:, gidx:gidx + 1], sring[:, gidx % 4, :], AX.X,
                             apply_absolute_value=True).then_inc(sem["s_sred"], 1)

            def _quantB(jt2, hc):
                if hc == 0:
                    v.wait_ge(blr[jt2 % 2], 16 * (jt2 // 2 + 1))
                    v.wait_ge(sem["s_scl1"], 1)
                    v.wait_ge(sem["s_Bt"], max(0, jt2 - 1))
                v.tensor_scalar(qfB[:, :], xln2in[:, jt2 % 2, hc, :],
                                invsx[:, :], MAGIC, ALU.mult, ALU.add)
                v.tensor_scalar(qfB[:, :], qfB[:, :],
                                MAGIC + QMAX, MAGIC - QMAX - 1.0, ALU.min, ALU.max)
                v.tensor_scalar(xqb[:, jt2 % 2, hc, :], qfB[:, :],
                                -MAGIC, None, ALU.add)
                if hc == DC - 1:
                    v.engine_nop().then_inc(sem["s_q"], 1)

            for hc in range(DC):
                _quantB(0, hc)
            for jt in range(1, NT + 1):
                for ic in range(IC):
                    _stt(jt - 1, ic)
                    if jt < NT and ic < DC:
                        _quantB(jt, ic)

            v.wait_ge(sem["s_sred"], IC * NT)
            v.reduce_max(lmax2[:, :], smaxs[:, :IC * NT], AX.X).then_inc(sem["s_lf2"], 1)
            v.wait_ge(sem["s_b2"], 32)
            v.reduce_max(row2[:1, 128:129], row2[:1, :128], AX.X).then_inc(sem["s_g2"], 1)
            v.wait_ge(sem["s_b2"], 64)
            v.tensor_tensor(s2c[:, :], gmax2[:, :], sigd[:, :], ALU.mult).then_inc(sem["s_bv2"], 1)
            v.wait_ge(sem["s_bv2"], 1)
            v.tensor_scalar(s2c[:, :], s2c[:, :], 1.0 / QMAX, None, ALU.mult).then_inc(sem["s_bv2"], 1)
            v.wait_ge(sem["s_bv2"], 2)
            v.reciprocal(invs2[:, :], s2c[:, :])
            v.tensor_scalar(escd[:, :], rsd[:, :], s2c[:, :], 1.0 / QMAX, ALU.mult, ALU.mult)
            v.engine_nop().then_inc(sem["s_scl2"], 1)

            def _evacC(mt2, hc):
                ghc = 8 * mt2 + hc
                v.wait_ge(sem["s_om"], ghc + 1)
                v.wait_ge(axr[ghc % 8], 16 * (NT + ghc // 8 + 1))
                v.wait_ge(ospr[ghc % 2], 16 * (ghc // 2))
                v.scalar_tensor_tensor(outring[:, ghc % 2, :], o_ps[hc % 4][:, :],
                                       escd[:, hc:hc + 1], xaring[:, ghc % 8, :],
                                       ALU.mult, ALU.add).then_inc(sem["s_ov"], 1)

            for mt in range(NT):
                v.wait_ge(sem["s_scl2"], 1)
                v.wait_ge(sem["s_om"], 8 * max(0, mt - 1))
                for ic in range(IC):
                    gic = IC * mt + ic
                    v.wait_ge(clr[gic % 3], 16 * (gic // 3 + 1))
                    v.tensor_scalar(qfC[:, :], sinring[:, gic % 3, :],
                                    invs2[:, :], MAGIC, ALU.mult, ALU.add)
                    v.tensor_scalar(qfC[:, :], qfC[:, :],
                                    MAGIC + QMAX, MAGIC - QMAX - 1.0, ALU.min, ALU.max)
                    v.tensor_scalar(hqb[:, mt % 2, ic, :], qfC[:, :],
                                    -MAGIC, None, ALU.add).then_inc(sem["s_qc"], 1)
                    if mt >= 1 and ic >= 4 and ic % 2 == 0 and (ic - 4) // 2 < DC:
                        _evacC(mt - 1, (ic - 4) // 2)
                v.engine_nop().then_inc(sem["s_qt"], 1)
            for hc in range(DC):
                _evacC(NT - 1, hc)

    return nc


_NC_CACHE = None


def kernel(L0_attn_x, L0_attn_W_in, L0_attn_b_in, L0_attn_log_A, L0_attn_B_vec,
           L0_attn_C, L0_attn_D, L0_ffn_clip_x, L0_ffn_clip_down,
           L0_ffn_w_gate, L0_ffn_w_up, L0_ffn_w_down):
    global _NC_CACHE
    x = np.asarray(L0_attn_x, dtype=np.float32)
    B, T, Dm = x.shape
    assert (B, T, Dm) == (16, 4096, 1024)

    if _NC_CACHE is None:
        _NC_CACHE = build_nc()
    nc = _NC_CACHE

    shared = {
        "W_in": np.ascontiguousarray(L0_attn_W_in, dtype=np.float32),
        "C_": np.ascontiguousarray(L0_attn_C, dtype=np.float32),
        "b_in": np.ascontiguousarray(L0_attn_b_in, dtype=np.float32),
        "log_A": np.ascontiguousarray(L0_attn_log_A, dtype=np.float32),
        "B_vec": np.ascontiguousarray(L0_attn_B_vec, dtype=np.float32),
        "D_": np.ascontiguousarray(L0_attn_D, dtype=np.float32),
        "clip_x": np.ascontiguousarray(np.asarray(L0_ffn_clip_x, dtype=np.float32).reshape(1, 1)),
        "clip_down": np.ascontiguousarray(np.asarray(L0_ffn_clip_down, dtype=np.float32).reshape(1, 1)),
        "wg": np.ascontiguousarray(L0_ffn_w_gate, dtype=np.float32),
        "wu": np.ascontiguousarray(L0_ffn_w_up, dtype=np.float32),
        "wd": np.ascontiguousarray(L0_ffn_w_down, dtype=np.float32),
        "eye": np.eye(128, dtype=np.float32),
    }
    in_maps = []
    for c in range(NCORES):
        xc = np.ascontiguousarray(x[2 * c:2 * c + 2].reshape(NTOK, Dm).T)
        m = dict(shared)
        m["xT"] = xc
        in_maps.append(m)

    res = run_bass_kernel_spmd(nc, in_maps, core_ids=list(range(NCORES)))
    outs = []
    for c in range(NCORES):
        yc = np.asarray(res.results[c]["out"])
        outs.append(np.ascontiguousarray(yc.T).reshape(2, SEQ, Dm))
    return np.concatenate(outs, axis=0).astype(np.float32)


# version 18 (final)
# speedup vs baseline: 1.0303x; 1.0303x over previous
import numpy as np
from contextlib import ExitStack
import concourse.bass as bass
import concourse.mybir as mybir
from concourse.bass_utils import run_bass_kernel_spmd

F32 = mybir.dt.float32
BF16 = mybir.dt.bfloat16
I32 = mybir.dt.int32
AF = mybir.ActivationFunctionType
ALU = mybir.AluOpType
AX = mybir.AxisListType

NCORES = 8
D = 1024
DI = 2816
NTOK = 8192
SEQ = 4096
TA = 512
NT = NTOK // TA
NSEQT = SEQ // TA
DC = D // 128
IC = DI // 128
EPS = 1e-5
QMAX = 127.0
MAGIC = 12582912.0


def build_nc(debug=False):
    nc = bass.Bass("TRN2", target_bir_lowering=False, detect_race_conditions=False)

    xT = nc.dram_tensor("xT", [D, NTOK], F32, kind="ExternalInput").ap()
    W_in = nc.dram_tensor("W_in", [D, D], F32, kind="ExternalInput").ap()
    C_ = nc.dram_tensor("C_", [D, D], F32, kind="ExternalInput").ap()
    b_in = nc.dram_tensor("b_in", [D], F32, kind="ExternalInput").ap()
    log_A = nc.dram_tensor("log_A", [D], F32, kind="ExternalInput").ap()
    B_vec = nc.dram_tensor("B_vec", [D], F32, kind="ExternalInput").ap()
    D_ = nc.dram_tensor("D_", [D], F32, kind="ExternalInput").ap()
    clip_x = nc.dram_tensor("clip_x", [1, 1], F32, kind="ExternalInput").ap()
    clip_down = nc.dram_tensor("clip_down", [1, 1], F32, kind="ExternalInput").ap()
    wg = nc.dram_tensor("wg", [DI, D], F32, kind="ExternalInput").ap()
    wu = nc.dram_tensor("wu", [DI, D], F32, kind="ExternalInput").ap()
    wd = nc.dram_tensor("wd", [D, DI], F32, kind="ExternalInput").ap()
    eye = nc.dram_tensor("eye", [128, 128], F32, kind="ExternalInput").ap()

    out_d = nc.dram_tensor("out", [D, NTOK], F32, kind="ExternalOutput").ap()

    dbg = "ExternalOutput" if debug else "Internal"
    xmid_d = nc.dram_tensor("xmid_d", [D, NTOK], F32, kind=dbg).ap()
    xln2_d = nc.dram_tensor("xln2_d", [D, NTOK], BF16, kind=dbg).ap()
    s_d = nc.dram_tensor("s_d", [DI, NTOK], F32, kind=dbg).ap()
    wgq_d = nc.dram_tensor("wgq_d", [DI, D], BF16, kind=dbg).ap()
    wuq_d = nc.dram_tensor("wuq_d", [DI, D], BF16, kind="Internal").ap()
    wdq_d = nc.dram_tensor("wdq_d", [D, DI], BF16, kind="Internal").ap()
    cc1_in = nc.dram_tensor("cc1_in", [128], F32, kind="Internal").ap()
    cc1_out = nc.dram_tensor("cc1_out", [128], F32, kind="Internal").ap()
    cc2_in = nc.dram_tensor("cc2_in", [128], F32, kind="Internal").ap()
    cc2_out = nc.dram_tensor("cc2_out", [128], F32, kind="Internal").ap()
    gs1_d = nc.dram_tensor("gs1_d", [1, 1], F32, kind="Internal").ap()
    gs2_d = nc.dram_tensor("gs2_d", [1, 1], F32, kind="Internal").ap()

    es = ExitStack()

    ab16 = es.enter_context(nc.sbuf_tensor("ab16", [128, 62592], BF16))
    _b = lambda o, n: ab16[:, o:o + n]
    winb = _b(0, 8192)
    cb = _b(8192, 8192)
    xln2in = _b(0, 8192).rearrange("p (b c t) -> p b c t", b=2, c=DC)
    xqb = _b(8192, 8192).rearrange("p (b c t) -> p b c t", b=2, c=DC)
    dgb = _b(16384, 1024)
    onesb = _b(17408, 128)
    AW = 17536
    xb2 = _b(AW + 0, 8192).rearrange("p (b c t) -> p b c t", b=2, c=DC)
    x2b = _b(AW + 8192, 4096).rearrange("p (c t) -> p c t", c=DC)
    tt = _b(AW + 40960, 4096).rearrange("p (c t) -> p c t", c=DC)
    xlnb2 = _b(AW + 12288, 8192).rearrange("p (b c t) -> p b c t", b=2, c=DC)
    hb = _b(AW + 24576, 4096).rearrange("p (c t) -> p c t", c=DC)
    xmb_sl = [_b(AW + 28672, 4096).rearrange("p (c t) -> p c t", c=DC),
              _b(AW + 20480, 4096).rearrange("p (c t) -> p c t", c=DC)]
    xm2b = _b(AW + 32768, 4096).rearrange("p (c t) -> p c t", c=DC)
    xl2 = _b(AW + 36864, 4096).rearrange("p (c t) -> p c t", c=DC)
    xl2f = _b(AW + 36864, 4096)
    wgqT = _b(AW, 22528).rearrange("p (h n) -> p h n", h=DC)
    wuqT = _b(AW + 22528, 22528).rearrange("p (h n) -> p h n", h=DC)
    wdqT = _b(AW, 22528).rearrange("p (i n) -> p i n", i=IC)
    hqb = _b(AW + 22528, 22528).rearrange("p (b i t) -> p b i t", b=2, i=IC)

    af32 = es.enter_context(nc.sbuf_tensor("af32", [128, 15872], F32))
    _f = lambda o, n: af32[:, o:o + n]
    acol = _f(0, 8)
    bvcol = _f(8, 8)
    bbvcol = _f(16, 8)
    dcol = _f(24, 8)
    rsg = _f(32, 22)
    rsu = _f(54, 22)
    rsd = _f(76, 8)
    escg = _f(84, 22)
    escu = _f(106, 22)
    escd = _f(128, 8)
    invsx = _f(136, 1)
    invs2 = _f(137, 1)
    gmax1 = _f(138, 1)
    gmax2 = _f(139, 1)
    sigx = _f(140, 1)
    sigd = _f(141, 1)
    sxc = _f(142, 1)
    s2c = _f(143, 1)
    clipx_t = _f(144, 1)
    clipd_t = _f(145, 1)
    lmax1 = _f(146, 1)
    lmax2 = _f(147, 1)
    invq = _f(148, 1)
    bcol = _f(152, 8)
    loc = _f(160, 16)
    ecol = _f(176, 8)
    smaxs = _f(192, 352)
    xaring = _f(1024, 4096).rearrange("p (r t) -> p r t", r=8)
    xmring = _f(5120, 2048).rearrange("p (r t) -> p r t", r=4)
    la1 = _f(7168, 512)
    rstd1 = _f(7680, 512)
    mr1 = _f(8192, 512)
    la2 = _f(8704, 512)
    rstd2 = _f(9216, 512)
    mr2 = _f(9728, 512)
    r1t = la1
    r2t = la2
    prepf = _f(10240, 5632).rearrange("p (r n) -> p r n", r=2)
    pf_s = [_f(10240 + 1024 * r, 1024) for r in range(4)]
    pf_A = _f(10240, 2816)
    pf_B = _f(13056, 2816)
    sgring = _f(10240, 1024).rearrange("p (r t) -> p r t", r=2)
    sring = _f(11264, 2048).rearrange("p (r t) -> p r t", r=4)
    sinring = _f(10240, 1536).rearrange("p (r t) -> p r t", r=3)
    xmin = _f(11776, 2048).rearrange("p (r t) -> p r t", r=4)
    outring = _f(13824, 1024).rearrange("p (r t) -> p r t", r=2)
    qfB = _f(14848, 512)
    qfC = _f(15360, 512)
    iv2 = _f(149, 1)
    epsc = _f(150, 1)
    magc = _f(544, 1)
    nmagc = _f(545, 1)
    nhalf = _f(546, 1)
    iv2s = _f(548, 64)

    brow = es.enter_context(nc.sbuf_tensor("brow", [1, 1024], BF16))
    one512 = es.enter_context(nc.sbuf_tensor("one512", [1, 512], BF16))
    row1 = es.enter_context(nc.sbuf_tensor("row1", [1, 160], F32))
    row2 = es.enter_context(nc.sbuf_tensor("row2", [1, 160], F32))

    pb = [es.enter_context(nc.psum_tensor(f"pb{i}", [128, 512], F32))
          for i in range(8)]
    m_ps1, q_ps1, m_ps2, q_ps2 = pb[0], pb[1], pb[2], pb[3]
    u_ps4 = [pb[4], pb[5], pb[6], pb[7]]
    y_ps = [pb[6], pb[7]]
    y_ps4 = [pb[6], pb[7], pb[4], pb[5]]
    g_ps = [pb[0], pb[1], pb[4], pb[5]]
    u2_ps = [pb[2], pb[3], pb[6], pb[7]]
    o_ps = [pb[0], pb[1], pb[2], pb[3]]

    semnames = [
        "s_sq1", "s_st1", "s_msq1", "s_rc1", "s_sr1", "s_mr1",
        "s_t", "s_xln", "s_u", "s_scan", "s_ymm", "s_res",
        "s_sq2", "s_st2", "s_msq2", "s_rc2", "s_sr2", "s_mr2",
        "s_t2", "s_xl2", "s_x2sp", "s_lmx",
        "s_lf1", "s_b1", "s_g1", "s_scl1",
        "s_q", "s_gm", "s_um", "s_sil", "s_stt", "s_sred",
        "s_Bt",
        "s_lf2", "s_b2", "s_g2", "s_scl2",
        "s_qc", "s_qt", "s_om", "s_ov",
        "s_pq", "s_pl5",
        "s_on", "s_wc1", "s_clp", "s_xmc", "s_ex", "s_dg", "s_sg", "s_cc1", "s_cc2", "s_vch", "s_bv1", "s_bv2",
    ]
    for base, n in [("s_axr", 8), ("s_xbr", 2), ("s_xmbr", 2), ("s_xmsp", 4),
                    ("s_blr", 2), ("s_sspr", 4), ("s_clr", 3), ("s_xmi", 4),
                    ("s_ospr", 2), ("s_pwr", 4), ("s_plr", 4), ("s_wtr", 4)]:
        semnames += [f"{base}{r}" for r in range(n)]
    sem = {n: es.enter_context(nc.semaphore(n)) for n in semnames}
    axr = [sem[f"s_axr{r}"] for r in range(8)]
    xbr = [sem[f"s_xbr{r}"] for r in range(2)]
    xmbr = [sem[f"s_xmbr{r}"] for r in range(2)]
    xmspr = [sem[f"s_xmsp{r}"] for r in range(4)]
    blr = [sem[f"s_blr{r}"] for r in range(2)]
    sspr = [sem[f"s_sspr{r}"] for r in range(4)]
    clr = [sem[f"s_clr{r}"] for r in range(3)]
    xmir = [sem[f"s_xmi{r}"] for r in range(4)]
    ospr = [sem[f"s_ospr{r}"] for r in range(2)]
    pwr = [sem[f"s_pwr{r}"] for r in range(4)]
    plr = [sem[f"s_plr{r}"] for r in range(4)]
    wtr = [sem[f"s_wtr{r}"] for r in range(4)]

    cols = lambda k: slice(k * TA, (k + 1) * TA)
    val = lambda t: 0 <= t < NT

    prep_tiles = ([("c", i) for i in range(DC)] + [("g", i) for i in range(IC)]
                  + [("u", i) for i in range(IC)] + [("d", i) for i in range(DC)])
    NSMALL = DC + 2 * IC

    def prep_ring(n):
        if n < NSMALL:
            return n % 4
        return 0 if (n - NSMALL) % 2 == 0 else 2

    def pfv(n):
        if n < NSMALL:
            return pf_s[n % 4]
        return pf_A if (n - NSMALL) % 2 == 0 else pf_B

    _plr_cnt, _pwr_cnt = {}, {}
    _ring_loads, _ring_stores = [0] * 4, [0] * 4
    _last_user = {0: None, 1: None, 2: None, 3: None}
    prep_load_guards = {}
    for _n, (_w, _i) in enumerate(prep_tiles):
        _r = prep_ring(_n)
        if _n < NSMALL:
            _ov = [_n % 4]
        elif _r == 0:
            _ov = [0, 1, 2]
        else:
            _ov = [2, 3]
        _qmax, _stg = None, {}
        for _sl in _ov:
            _p = _last_user[_sl]
            if _p is not None:
                _qmax = _p if _qmax is None else max(_qmax, _p)
                if prep_tiles[_p][0] != "c":
                    _pr = prep_ring(_p)
                    _stg[_pr] = max(_stg.get(_pr, 0), _pwr_cnt[_p])
        prep_load_guards[_n] = (_qmax, sorted(_stg.items()))
        _ring_loads[_r] += 1
        _plr_cnt[_n] = _ring_loads[_r]
        if _w != "c":
            _ring_stores[_r] += 1
            _pwr_cnt[_n] = _ring_stores[_r]
        for _sl in _ov:
            _last_user[_sl] = _n
    PSTORE_RING = list(_ring_stores)

    def pwr_counts_through(m):
        c = [0] * 4
        for _n2 in range(m + 1):
            if prep_tiles[_n2][0] != "c":
                c[prep_ring(_n2)] += 1
        return c
    PREP_PER_IT = 4
    NPREP = len(prep_tiles)
    def prep_chunk(it):
        if it <= 12:
            return range(PREP_PER_IT * it, min(NSMALL, PREP_PER_IT * (it + 1)))
        n = NSMALL + (it - 13)
        return range(n, n + 1) if n < NPREP else range(0, 0)
    PREP_LAST_IT = 13 + (NPREP - NSMALL) - 1
    prep_src = {"g": wg, "u": wu, "d": wd, "c": C_}
    prep_dst = {"g": wgq_d, "u": wuq_d, "d": wdq_d}
    prep_rs = {"g": rsg, "u": rsu, "d": rsd}
    prep_w = {"g": 1024, "u": 1024, "d": 2816, "c": 1024}

    vch_after = {}
    _cnt = 0
    for _n, (_w, _i) in enumerate(prep_tiles):
        if _w != "c":
            _cnt += 3
        vch_after[_n] = _cnt

    def prev_bank_user(gidx):
        ic = gidx % IC
        jt = gidx // IC
        if ic >= 4:
            return gidx - 4
        if jt == 0:
            return -1
        return IC * (jt - 1) + [20, 21, 18, 19][ic]

    with es, nc.Block() as block:

        @block.sync
        def _(sync):
            sync.dma_start(out=acol[:, :], in_=log_A.rearrange("(c p) -> p c", p=128), allow_slow_non_contiguous=True).then_inc(sem["s_pl5"], 16)
            sync.dma_start(out=bvcol[:, :], in_=B_vec.rearrange("(c p) -> p c", p=128), allow_slow_non_contiguous=True).then_inc(sem["s_pl5"], 16)
            sync.dma_start(out=bcol[:, :], in_=b_in.rearrange("(c p) -> p c", p=128), allow_slow_non_contiguous=True).then_inc(sem["s_pl5"], 16)
            sync.dma_start(out=dcol[:, :], in_=D_.rearrange("(c p) -> p c", p=128), allow_slow_non_contiguous=True).then_inc(sem["s_pl5"], 16)
            sync.dma_start(out=pf_s[0][:, :128], in_=eye).then_inc(sem["s_pl5"], 16)

            def _prep_load_sync(n):
                w, i = prep_tiles[n]
                qmax, stg = prep_load_guards[n]
                if qmax is not None:
                    sync.wait_ge(sem["s_pq"], qmax + 1)
                for rr, cc in stg:
                    sync.wait_ge(pwr[rr], 16 * cc)
                sync.dma_start(out=pfv(n)[:, :],
                               in_=prep_src[w][bass.ts(i, 128), :]).then_inc(plr[prep_ring(n)], 16)

            for it in range(NT + 3):
                if it == 17:
                    _cg = pwr_counts_through(29)
                    for rr in range(4):
                        if _cg[rr]:
                            sync.wait_ge(pwr[rr], 16 * _cg[rr])
                    sync.wait_ge(sem["s_st1"], NT)
                    sync.wait_ge(sem["s_t"], NT)
                    for hc in range(4):
                        sync.dma_start_transpose(wgqT[:, hc, :], wgq_d[:, bass.ts(hc, 128)]).then_inc(wtr[hc % 4], 16)
                if it == 18:
                    sync.wait_ge(sem["s_u"], 8 * NT)
                    sync.wait_ge(sem["s_ymm"], 8 * NT)
                    for hc in range(4, 7):
                        sync.dma_start_transpose(wgqT[:, hc, :], wgq_d[:, bass.ts(hc, 128)]).then_inc(wtr[hc % 4], 16)
                for n in prep_chunk(it):
                    if n >= NSMALL:
                        _prep_load_sync(n)
                k2 = it - 2
                if val(k2):
                    for dc in range(DC):
                        gdc = 8 * k2 + dc
                        sync.wait_ge(sem["s_res"], gdc + 1)
                        sync.dma_start(out=xmid_d[bass.ts(dc, 128), cols(k2)],
                                       in_=xmring[:, gdc % 4, :]).then_inc(xmspr[gdc % 4], 16)
                k_ld = it - 1
                if val(k_ld):
                    for dc in range(DC):
                        gdc = 8 * k_ld + dc
                        sync.wait_ge(sem["s_res"], max(0, gdc - 7))
                        sync.dma_start(out=xaring[:, gdc % 8, :],
                                       in_=xT[bass.ts(dc, 128), cols(k_ld)]).then_inc(axr[gdc % 8], 16)
                k_sp = it - 3
                if val(k_sp):
                    sync.wait_ge(sem["s_xl2"], k_sp + 1)
                    sync.dma_start(
                        out=xln2_d[:, cols(k_sp)].rearrange("(c p) t -> p c t", p=128),
                        in_=xl2[:, :, :]).then_inc(sem["s_x2sp"], 16)

            for it2 in range(NT + 3, PREP_LAST_IT + 1):
                for n in prep_chunk(it2):
                    if n >= NSMALL:
                        _prep_load_sync(n)

            sync.wait_ge(sem["s_ymm"], 8 * NT)
            for jt in range(2):
                sync.wait_ge(sem["s_x2sp"], 16 * (jt + 1))
                sync.dma_start(out=xln2in[:, jt % 2, :, :],
                               in_=xln2_d[:, cols(jt)].rearrange("(c p) t -> p c t", p=128)).then_inc(blr[jt % 2], 16)
            sync.wait_ge(sem["s_x2sp"], 16 * NT)
            sync.wait_ge(sem["s_res"], 8 * NT)
            sync.wait_ge(sem["s_lmx"], NT)
            _cgu = pwr_counts_through(NSMALL - 1)
            for rr in range(4):
                if _cgu[rr]:
                    sync.wait_ge(pwr[rr], 16 * _cgu[rr])
            for hc in range(7, DC):
                j = hc
                sync.wait_ge(wtr[j % 4], 16 * (j // 4))
                sync.dma_start_transpose(wgqT[:, hc, :], wgq_d[:, bass.ts(hc, 128)]).then_inc(wtr[j % 4], 16)
            for hc in range(DC):
                j = 8 + hc
                sync.wait_ge(wtr[j % 4], 16 * (j // 4))
                sync.dma_start_transpose(wuqT[:, hc, :], wuq_d[:, bass.ts(hc, 128)]).then_inc(wtr[j % 4], 16)

            for jt in range(2, NT):
                sync.wait_ge(sem["s_q"], max(0, jt - 1))
                sync.dma_start(out=xln2in[:, jt % 2, :, :],
                               in_=xln2_d[:, cols(jt)].rearrange("(c p) t -> p c t", p=128)).then_inc(blr[jt % 2], 16)

            sync.wait_ge(sem["s_Bt"], NT)
            for rr in range(4):
                if PSTORE_RING[rr]:
                    sync.wait_ge(pwr[rr], 16 * PSTORE_RING[rr])
            for ic in range(IC):
                j = 16 + ic
                sync.wait_ge(wtr[j % 4], 16 * (j // 4))
                sync.dma_start_transpose(wdqT[:, ic, :], wdq_d[:, bass.ts(ic, 128)]).then_inc(wtr[j % 4], 16)

            for mt in range(NT):
                for ic in range(IC):
                    gic = IC * mt + ic
                    sync.wait_ge(sem["s_qc"], max(0, gic - 2))
                    sync.wait_ge(sspr[gic % 4], 16 * (gic // 4 + 1))
                    sync.dma_start(out=sinring[:, gic % 3, :],
                                   in_=s_d[bass.ts(ic, 128), cols(mt)]).then_inc(clr[gic % 3], 16)

        @block.gpsimd
        def _(g):
            g.memset(onesb[:, :], 1.0 / 1024.0).then_inc(sem["s_on"], 1)
            g.memset(one512[:1, :], 1.0).then_inc(sem["s_on"], 1)
            g.memset(magc[:, :], MAGIC).then_inc(sem["s_on"], 1)
            g.memset(nmagc[:, :], -MAGIC).then_inc(sem["s_on"], 1)
            g.memset(nhalf[:, :], -0.5).then_inc(sem["s_on"], 1)
            g.dma_start(out=brow[:1, :], in_=b_in.rearrange("(o d) -> o d", o=1)).then_inc(sem["s_clp"], 16)
            g.dma_start(out=clipx_t[:, :], in_=clip_x.broadcast_to([128, 1])).then_inc(sem["s_clp"], 16)
            g.dma_start(out=clipd_t[:, :], in_=clip_down.broadcast_to([128, 1])).then_inc(sem["s_clp"], 16)

            for it in range(NT + 3):
                k0 = it
                if val(k0):
                    g.wait_ge(sem["s_sq1"], max(0, k0 - 1))
                    g.wait_ge(sem["s_st1"], max(0, k0 - 1))
                    g.dma_start(out=xb2[:, k0 % 2, :, :],
                                in_=xT[:, cols(k0)].rearrange("(c p) t -> p c t", p=128)).then_inc(xbr[k0 % 2], 16)
                if it < 2:
                    for dc in range(4 * it, 4 * it + 4):
                        g.dma_start(out=winb[:, bass.ts(dc, 1024)],
                                    in_=W_in[bass.ts(dc, 128), :]).then_inc(sem["s_wc1"], 16)
                k3 = it - 3
                if val(k3):
                    g.wait_ge(sem["s_xmc"], 8 * (k3 + 1))
                    g.wait_ge(sem["s_sr2"], k3 + 1)
                    g.wait_ge(sem["s_st2"], k3 + 1)
                    g.wait_ge(sem["s_x2sp"], 16 * k3)
                    g.wait_ge(sem["s_lmx"], k3)
                    rb2 = rstd2.rearrange("p (o t) -> p o t", o=1).broadcast_to([128, DC, TA])
                    g.tensor_tensor(xl2[:, :, :], xmb_sl[k3 % 2][:, :, :], rb2, ALU.mult).then_inc(sem["s_t2"], 1)
                k1g = it - 1
                if val(k1g):
                    g.wait_ge(sem["s_sr1"], k1g + 1)
                    g.wait_ge(xbr[k1g % 2], 16 * (k1g // 2 + 1))
                    rb1 = rstd1.rearrange("p (o t) -> p o t", o=1).broadcast_to([128, DC, TA])
                    g.tensor_tensor(tt[:, :, :], xb2[:, k1g % 2, :, :], rb1,
                                    ALU.mult).then_inc(sem["s_t"], 1)
                    g.wait_ge(sem["s_mr1"], k1g + 1)
                    g.wait_ge(sem["s_u"], 8 * max(0, k1g - 1))
                    g.wait_ge(sem["s_ymm"], 8 * max(0, k1g - 1))
                    mb1 = mr1.rearrange("p (o t) -> p o t", o=1).broadcast_to([128, DC, TA])
                    g.tensor_tensor(xlnb2[:, k1g % 2, :, :], tt[:, :, :], mb1,
                                    ALU.subtract).then_inc(sem["s_xln"], 1)
                for n in prep_chunk(it):
                    w, i = prep_tiles[n]
                    if w == "c":
                        continue
                    g.wait_ge(sem["s_pq"], n + 1)
                    g.wait_ge(pwr[prep_ring(n)], 16 * (_pwr_cnt[n] - 1))
                    g.dma_start(out=prep_dst[w][bass.ts(i, 128), :],
                                in_=pfv(n)[:, :]).then_inc(pwr[prep_ring(n)], 16)

            g.wait_ge(sem["s_lf1"], 1)
            g.dma_start(out=cc1_in, in_=lmax1[:, :]).then_inc(sem["s_b1"], 16)
            g.wait_ge(sem["s_b1"], 16)
            g.collective_compute("AllReduce", ALU.max,
                                 replica_groups=[list(range(NCORES))],
                                 ins=[cc1_in], outs=[cc1_out]).then_inc(sem["s_cc1"], 1)
            g.wait_ge(sem["s_cc1"], 1)
            g.dma_start(out=row1[:1, :128], in_=cc1_out).then_inc(sem["s_b1"], 16)

            for it2 in range(NT + 3, PREP_LAST_IT + 1):
                for n in prep_chunk(it2):
                    w, i = prep_tiles[n]
                    g.wait_ge(sem["s_pq"], n + 1)
                    g.wait_ge(pwr[prep_ring(n)], 16 * (_pwr_cnt[n] - 1))
                    g.dma_start(out=prep_dst[w][bass.ts(i, 128), :],
                                in_=pfv(n)[:, :]).then_inc(pwr[prep_ring(n)], 16)

            g.wait_ge(sem["s_g1"], 1)
            g.dma_start(out=gs1_d, in_=row1[:1, 128:129]).then_inc(sem["s_b1"], 16)
            g.wait_ge(sem["s_b1"], 48)
            g.dma_start(out=gmax1[:, :], in_=gs1_d.broadcast_to([128, 1])).then_inc(sem["s_b1"], 16)

            g.wait_ge(sem["s_lf2"], 1)
            g.dma_start(out=cc2_in, in_=lmax2[:, :]).then_inc(sem["s_b2"], 16)
            g.wait_ge(sem["s_b2"], 16)
            g.collective_compute("AllReduce", ALU.max,
                                 replica_groups=[list(range(NCORES))],
                                 ins=[cc2_in], outs=[cc2_out]).then_inc(sem["s_cc2"], 1)
            g.wait_ge(sem["s_cc2"], 1)
            g.dma_start(out=row2[:1, :128], in_=cc2_out).then_inc(sem["s_b2"], 16)
            g.wait_ge(sem["s_g2"], 1)
            g.dma_start(out=gs2_d, in_=row2[:1, 128:129]).then_inc(sem["s_b2"], 16)
            g.wait_ge(sem["s_b2"], 48)
            g.dma_start(out=gmax2[:, :], in_=gs2_d.broadcast_to([128, 1])).then_inc(sem["s_b2"], 16)

            for mt in range(NT):
                for hc in range(DC):
                    ghc = 8 * mt + hc
                    g.wait_ge(sem["s_ov"], max(0, ghc - 7))
                    g.dma_start(out=xaring[:, ghc % 8, :],
                                in_=xmid_d[bass.ts(hc, 128), cols(mt)]).then_inc(axr[ghc % 8], 16)
            g.wait_ge(ospr[0], 16 * 8 * NT // 2)
            g.wait_ge(ospr[1], 16 * 8 * NT // 2)

        @block.tensor
        def _(t):
            t.wait_ge(sem["s_on"], 1)
            first_u = [True]
            first_y = [True]
            for it in range(NT + 3):
                k3 = it - 3
                if val(k3):
                    t.wait_ge(sem["s_sq2"], k3 + 1)
                    t.wait_ge(sem["s_xmc"], 8 * (k3 + 1))
                    t.wait_ge(sem["s_mr2"], k3)
                    for c in range(DC):
                        t.matmul(m_ps2[:, :], onesb[:, :], xmb_sl[k3 % 2][:, c, :],
                                 start=(c == 0), stop=(c == DC - 1))
                    for c in range(DC):
                        mm = t.matmul(q_ps2[:, :], onesb[:, :], xm2b[:, c, :],
                                      start=(c == 0), stop=(c == DC - 1))
                    mm.then_inc(sem["s_st2"], 1)
                k1 = it - 1
                if val(k1):
                    t.wait_ge(sem["s_sq1"], k1 + 1)
                    t.wait_ge(xbr[k1 % 2], 16 * (k1 // 2 + 1))
                    t.wait_ge(sem["s_mr1"], k1)
                    for c in range(DC):
                        t.matmul(m_ps1[:, :], onesb[:, :], xb2[:, k1 % 2, c, :],
                                 start=(c == 0), stop=(c == DC - 1))
                    for c in range(DC):
                        mm = t.matmul(q_ps1[:, :], onesb[:, :], x2b[:, c, :],
                                      start=(c == 0), stop=(c == DC - 1))
                    mm.then_inc(sem["s_st1"], 1)
                k2 = it - 2
                if val(k2):
                    if first_u[0]:
                        t.wait_ge(sem["s_wc1"], 16 * 8)
                        t.wait_ge(sem["s_clp"], 48)
                        t.wait_ge(sem["s_on"], 2)
                        first_u[0] = False
                    t.wait_ge(sem["s_xln"], k2 + 1)
                    for ec in range(DC):
                        t.wait_ge(sem["s_scan"], max(0, 8 * k2 + ec - 3))
                        if ec < 4:
                            t.wait_ge(sem["s_res"], max(0, 8 * k2 + (-1, 0, -3, -2)[ec]))
                        for dc in range(DC):
                            t.matmul(u_ps4[ec % 4][:, :],
                                     winb[:, dc * 1024 + ec * 128: dc * 1024 + (ec + 1) * 128],
                                     xlnb2[:, k2 % 2, dc, :],
                                     start=(dc == 0), stop=False)
                        mm = t.matmul(u_ps4[ec % 4][:, :],
                                      brow[:1, bass.ts(ec, 128)], one512[:1, :],
                                      start=False, stop=True)
                        mm.then_inc(sem["s_u"], 1)
                if val(k2):
                    if first_y[0]:
                        t.wait_ge(sem["s_pq"], 8)
                        t.wait_ge(sem["s_dg"], 1)
                        first_y[0] = False
                    for dc in range(DC):
                        t.wait_ge(sem["s_res"], max(0, 8 * k2 + dc - 3))
                        if dc == 0:
                            t.wait_ge(sem["s_scan"], 8 * k2 + 8)
                        t.matmul(y_ps4[dc % 4][:, :], dgb[:, bass.ts(dc, 128)],
                                 xlnb2[:, k2 % 2, dc, :], start=True, stop=False)
                        for sc in range(DC):
                            mm = t.matmul(y_ps4[dc % 4][:, :],
                                          cb[:, sc * 1024 + dc * 128: sc * 1024 + (dc + 1) * 128],
                                          hb[:, sc, :], start=False, stop=(sc == DC - 1))
                        mm.then_inc(sem["s_ymm"], 1)

            t.wait_ge(sem["s_mr1"], NT)
            t.wait_ge(sem["s_mr2"], NT)
            t.wait_ge(sem["s_scan"], 8 * NT)
            t.wait_ge(sem["s_res"], 8 * NT)
            for jt in range(NT):
                t.wait_ge(sem["s_q"], jt + 1)
                for ic in range(IC):
                    gidx = IC * jt + ic
                    t.wait_ge(sem["s_sil"], max(0, gidx - 1))
                    for hc in range(DC):
                        if jt == 0 and ic == 0:
                            t.wait_ge(wtr[hc % 4], 16 * (hc // 4 + 1))
                        mm = t.matmul(g_ps[ic % 2][:, :],
                                      wgqT[:, hc, bass.ts(ic, 128)],
                                      xqb[:, jt % 2, hc, :],
                                      start=(hc == 0), stop=(hc == DC - 1))
                    mm.then_inc(sem["s_gm"], 1)
                    t.wait_ge(sem["s_stt"], max(0, gidx - 1))
                    for hc in range(DC):
                        if jt == 0 and ic == 0:
                            t.wait_ge(wtr[hc % 4], 16 * (hc // 4 + 3))
                        mm = t.matmul(u2_ps[ic % 2][:, :],
                                      wuqT[:, hc, bass.ts(ic, 128)],
                                      xqb[:, jt % 2, hc, :],
                                      start=(hc == 0), stop=(hc == DC - 1))
                    mm.then_inc(sem["s_um"], 1)
                t.nop().then_inc(sem["s_Bt"], 1)

            t.wait_ge(sem["s_sil"], IC * NT)
            t.wait_ge(sem["s_stt"], IC * NT)
            for mt in range(NT):
                for hc in range(DC):
                    ghc = 8 * mt + hc
                    t.wait_ge(sem["s_ov"], max(0, ghc - 3))
                    for ic in range(IC):
                        if hc == 0:
                            t.wait_ge(sem["s_qc"], IC * mt + ic + 1)
                        if mt == 0 and hc == 0:
                            t.wait_ge(wtr[ic % 4], 16 * (ic // 4 + 5))
                        mm = t.matmul(o_ps[hc % 4][:, :],
                                      wdqT[:, ic, bass.ts(hc, 128)],
                                      hqb[:, mt % 2, ic, :],
                                      start=(ic == 0), stop=(ic == IC - 1))
                    mm.then_inc(sem["s_om"], 1)

        @block.scalar
        def _(s):
            s.wait_ge(xbr[0], 16)
            s.activation(x2b[:, :, :], xb2[:, 0, :, :], AF.Square).then_inc(sem["s_sq1"], 1)

            s.wait_ge(sem["s_pl5"], 80)
            s.wait_ge(sem["s_on"], 5)
            s.activation(ecol[:, :], acol[:, :], AF.Exp).then_inc(sem["s_ex"], 1)
            s.wait_ge(sem["s_clp"], 48)
            s.activation(sigx[:, :], clipx_t[:, :], AF.Sigmoid)
            s.activation(sigd[:, :], clipd_t[:, :], AF.Sigmoid).then_inc(sem["s_sg"], 1)

            def _prep_quant_s(n):
                w, i = prep_tiles[n]
                if w == "c":
                    return
                s.wait_ge(sem["s_on"], 4)
                s.wait_ge(sem["s_vch"], vch_after[n])
                s.activation(pfv(n)[:, :], pfv(n)[:, :],
                             AF.Identity, scale=iv2s[:, n:n + 1], bias=magc[:, :1])
                s.activation(pfv(n)[:, :], pfv(n)[:, :],
                             AF.Identity, bias=nmagc[:, :1]).then_inc(sem["s_pq"], 1)

            def _prep_load_s(n):
                w, i = prep_tiles[n]
                if n == 0:
                    s.wait_ge(sem["s_dg"], 1)
                qmax, stg = prep_load_guards[n]
                if qmax is not None:
                    s.wait_ge(sem["s_pq"], qmax + 1)
                for rr, cc in stg:
                    s.wait_ge(pwr[rr], 16 * cc)
                s.dma_start(out=pfv(n)[:, :],
                            in_=prep_src[w][bass.ts(i, 128), :]).then_inc(plr[prep_ring(n)], 16)

            for it in range(NT + 3):
                k0, k1, k2, k3 = it, it - 1, it - 2, it - 3
                for n in prep_chunk(it):
                    if n < NSMALL:
                        _prep_load_s(n)
                if val(k1):
                    s.wait_ge(sem["s_st1"], k1 + 1)
                    s.activation(la1[:, :], m_ps1[:, :], AF.Square).then_inc(sem["s_msq1"], 1)
                if val(k3):
                    s.wait_ge(sem["s_st2"], k3 + 1)
                    s.activation(la2[:, :], m_ps2[:, :], AF.Square).then_inc(sem["s_msq2"], 1)
                if val(k1):
                    s.wait_ge(sem["s_rc1"], k1 + 1)
                    s.wait_ge(sem["s_t"], k1)
                    s.wait_ge(sem["s_mr1"], k1)
                    s.activation(rstd1[:, :], la1[:, :], AF.Ln)
                    s.activation(rstd1[:, :], rstd1[:, :], AF.Exp,
                                 scale=nhalf[:, :1]).then_inc(sem["s_sr1"], 1)
                if val(k3):
                    s.wait_ge(sem["s_rc2"], k3 + 1)
                    s.wait_ge(sem["s_t2"], k3)
                    s.wait_ge(sem["s_mr2"], k3)
                    s.activation(rstd2[:, :], la2[:, :], AF.Ln)
                    s.activation(rstd2[:, :], rstd2[:, :], AF.Exp,
                                 scale=nhalf[:, :1]).then_inc(sem["s_sr2"], 1)
                if val(k0) and k0 > 0:
                    s.wait_ge(xbr[k0 % 2], 16 * (k0 // 2 + 1))
                    s.wait_ge(sem["s_st1"], k0)
                    s.activation(x2b[:, :, :], xb2[:, k0 % 2, :, :], AF.Square).then_inc(sem["s_sq1"], 1)
                if val(k2):
                    s.wait_ge(sem["s_st2"], k2)
                    for dc in range(DC):
                        s.wait_ge(sem["s_xmc"], 8 * k2 + dc + 1)
                        a = s.activation(xm2b[:, dc, :], xmb_sl[k2 % 2][:, dc, :], AF.Square)
                        if dc == DC - 1:
                            a.then_inc(sem["s_sq2"], 1)
                for n in prep_chunk(it):
                    _prep_quant_s(n)

            for it2 in range(NT + 3, PREP_LAST_IT + 1):
                for n in prep_chunk(it2):
                    _prep_quant_s(n)

            s.wait_ge(sem["s_scl1"], 1)
            def _spill_s(g2):
                jt2, ic2 = divmod(g2, IC)
                s.wait_ge(sem["s_sred"], g2 + 1)
                s.dma_start(out=s_d[bass.ts(ic2, 128), cols(jt2)],
                            in_=sring[:, g2 % 4, :]).then_inc(sspr[g2 % 4], 16)
            for jt in range(NT):
                for ic in range(IC):
                    gidx = IC * jt + ic
                    s.wait_ge(sem["s_gm"], gidx + 1)
                    s.wait_ge(sem["s_stt"], max(0, gidx - 1))
                    s.activation(sgring[:, ic % 2, :], g_ps[ic % 2][:, :], AF.Silu,
                                 scale=escg[:, ic:ic + 1]).then_inc(sem["s_sil"], 1)
                    if gidx >= 4:
                        _spill_s(gidx - 4)
            for g2 in range(IC * NT - 4, IC * NT):
                _spill_s(g2)

            for mt in range(NT):
                for hc in range(DC):
                    ghc = 8 * mt + hc
                    s.wait_ge(sem["s_ov"], ghc + 1)
                    s.dma_start(out=out_d[bass.ts(hc, 128), cols(mt)],
                                in_=outring[:, ghc % 2, :]).then_inc(ospr[ghc % 2], 16)
            s.wait_ge(ospr[0], 16 * 8 * NT // 2)
            s.wait_ge(ospr[1], 16 * 8 * NT // 2)

        @block.vector
        def _(v):
            v.wait_ge(sem["s_ex"], 1)
            v.wait_ge(sem["s_pl5"], 80)
            v.tensor_scalar(acol[:, :], ecol[:, :], -1.0, None, ALU.mult)
            v.tensor_tensor(bbvcol[:, :], bcol[:, :], bvcol[:, :], ALU.mult)
            for dc in range(DC):
                v.tensor_scalar(dgb[:, bass.ts(dc, 128)], pf_s[0][:, :128],
                                dcol[:, dc:dc + 1], None, ALU.mult)
            v.engine_nop().then_inc(sem["s_dg"], 1)
            vch = [0]
            def _prep_chain(n):
                w, i = prep_tiles[n]
                v.wait_ge(plr[prep_ring(n)], 16 * _plr_cnt[n])
                if w == "c":
                    v.tensor_scalar(cb[:, bass.ts(i, 1024)], pfv(n)[:, :],
                                    bvcol[:, i:i + 1], None, ALU.mult).then_inc(sem["s_pq"], 1)
                    return
                v.reduce_max(prep_rs[w][:, i:i + 1], pfv(n)[:, :], AX.X,
                             apply_absolute_value=True).then_inc(sem["s_vch"], 1)
                vch[0] += 1
                v.wait_ge(sem["s_vch"], vch[0])
                v.reciprocal(invq[:, :], prep_rs[w][:, i:i + 1]).then_inc(sem["s_vch"], 1)
                vch[0] += 1
                v.wait_ge(sem["s_vch"], vch[0])
                v.tensor_scalar(iv2s[:, n:n + 1], invq[:, :], QMAX, None, ALU.mult).then_inc(sem["s_vch"], 1)
                vch[0] += 1

            for it in range(NT + 3):
                k0, k1, k2, k3 = it, it - 1, it - 2, it - 3
                if val(k1):
                    v.wait_ge(sem["s_msq1"], k1 + 1)
                    v.scalar_tensor_tensor(la1[:, :], q_ps1[:, :], EPS, la1[:, :],
                                           ALU.add, ALU.subtract).then_inc(sem["s_rc1"], 1)
                if val(k3):
                    v.wait_ge(sem["s_msq2"], k3 + 1)
                    v.scalar_tensor_tensor(la2[:, :], q_ps2[:, :], EPS, la2[:, :],
                                           ALU.add, ALU.subtract).then_inc(sem["s_rc2"], 1)
                if val(k1):
                    v.wait_ge(sem["s_sr1"], k1 + 1)
                    v.tensor_tensor(mr1[:, :], m_ps1[:, :], rstd1[:, :], ALU.mult).then_inc(sem["s_mr1"], 1)
                if val(k2):
                    for ec in range(DC):
                        v.wait_ge(sem["s_u"], 8 * k2 + ec + 1)
                        init = 0.0 if (k2 % NSEQT == 0) else hb[:, ec, TA - 1:TA]
                        v.tensor_tensor_scan(hb[:, ec, :],
                                             acol[:, ec:ec + 1].broadcast_to([128, TA]),
                                             u_ps4[ec % 4][:, :], init,
                                             ALU.mult, ALU.add).then_inc(sem["s_scan"], 1)
                if val(k3):
                    v.wait_ge(sem["s_sr2"], k3 + 1)
                    v.tensor_tensor(mr2[:, :], m_ps2[:, :], rstd2[:, :], ALU.mult).then_inc(sem["s_mr2"], 1)

                def _xl2_sub():
                    v.wait_ge(sem["s_t2"], k3 + 1)
                    mb2 = mr2.rearrange("p (o t) -> p o t", o=1).broadcast_to([128, DC, TA])
                    v.tensor_tensor(xl2[:, :, :], xl2[:, :, :], mb2, ALU.subtract).then_inc(sem["s_xl2"], 1)

                def _xl2_red():
                    v.reduce_max(loc[:, k3:k3 + 1], xl2f[:, :], AX.X,
                                 apply_absolute_value=True).then_inc(sem["s_lmx"], 1)

                if val(k2):
                    v.wait_ge(sem["s_sq2"], max(0, k2 - 1))
                    v.wait_ge(sem["s_st2"], max(0, k2 - 1))
                    v.wait_ge(sem["s_t2"], max(0, k2 - 1))
                    for dc in range(DC):
                        gdc = 8 * k2 + dc
                        v.wait_ge(sem["s_ymm"], gdc + 1)
                        v.wait_ge(axr[gdc % 8], 16 * (gdc // 8 + 1))
                        v.wait_ge(xmspr[gdc % 4], 16 * (gdc // 4))
                        v.tensor_tensor(xmring[:, gdc % 4, :], y_ps4[dc % 4][:, :],
                                        xaring[:, gdc % 8, :], ALU.add).then_inc(sem["s_res"], 1)
                        v.tensor_copy(xmb_sl[k2 % 2][:, dc, :],
                                      xmring[:, gdc % 4, :]).then_inc(sem["s_xmc"], 1)
                        if val(k3) and dc == 1:
                            _xl2_sub()
                        if val(k3) and dc == 4:
                            _xl2_red()
                elif val(k3):
                    _xl2_sub()
                    _xl2_red()
                for n in prep_chunk(it):
                    _prep_chain(n)

            v.wait_ge(sem["s_lmx"], NT)
            v.reduce_max(lmax1[:, :], loc[:, :NT], AX.X).then_inc(sem["s_lf1"], 1)

            for it2 in range(NT + 3, PREP_LAST_IT + 1):
                for n in prep_chunk(it2):
                    _prep_chain(n)

            v.wait_ge(sem["s_b1"], 32)
            v.reduce_max(row1[:1, 128:129], row1[:1, :128], AX.X).then_inc(sem["s_g1"], 1)
            v.wait_ge(sem["s_b1"], 64)
            v.wait_ge(sem["s_sg"], 1)
            v.tensor_tensor(sxc[:, :], gmax1[:, :], sigx[:, :], ALU.mult).then_inc(sem["s_bv1"], 1)
            v.wait_ge(sem["s_bv1"], 1)
            v.tensor_scalar(sxc[:, :], sxc[:, :], 1.0 / QMAX, None, ALU.mult).then_inc(sem["s_bv1"], 1)
            v.wait_ge(sem["s_bv1"], 2)
            v.reciprocal(invsx[:, :], sxc[:, :])
            v.tensor_scalar(escg[:, :], rsg[:, :], sxc[:, :], 1.0 / QMAX, ALU.mult, ALU.mult)
            v.tensor_scalar(escu[:, :], rsu[:, :], sxc[:, :], 1.0 / QMAX, ALU.mult, ALU.mult)
            v.engine_nop().then_inc(sem["s_scl1"], 1)

            def _stt(jt2, ic):
                gidx = IC * jt2 + ic
                v.wait_ge(sem["s_um"], gidx + 1)
                v.wait_ge(sem["s_sil"], gidx + 1)
                v.wait_ge(sspr[gidx % 4], 16 * (gidx // 4))
                v.scalar_tensor_tensor(sring[:, gidx % 4, :], u2_ps[ic % 2][:, :],
                                       escu[:, ic:ic + 1], sgring[:, ic % 2, :],
                                       ALU.mult, ALU.mult).then_inc(sem["s_stt"], 1)
                v.reduce_max(smaxs[:, gidx:gidx + 1], sring[:, gidx % 4, :], AX.X,
                             apply_absolute_value=True).then_inc(sem["s_sred"], 1)

            def _quantB(jt2, hc):
                if hc == 0:
                    v.wait_ge(blr[jt2 % 2], 16 * (jt2 // 2 + 1))
                    v.wait_ge(sem["s_scl1"], 1)
                    v.wait_ge(sem["s_Bt"], max(0, jt2 - 1))
                v.tensor_scalar(qfB[:, :], xln2in[:, jt2 % 2, hc, :],
                                invsx[:, :], MAGIC, ALU.mult, ALU.add)
                v.tensor_scalar(qfB[:, :], qfB[:, :],
                                MAGIC + QMAX, MAGIC - QMAX - 1.0, ALU.min, ALU.max)
                v.tensor_scalar(xqb[:, jt2 % 2, hc, :], qfB[:, :],
                                -MAGIC, None, ALU.add)
                if hc == DC - 1:
                    v.engine_nop().then_inc(sem["s_q"], 1)

            for hc in range(DC):
                _quantB(0, hc)
            for jt in range(1, NT + 1):
                for ic in range(IC):
                    _stt(jt - 1, ic)
                    if jt < NT and ic < DC:
                        _quantB(jt, ic)

            v.wait_ge(sem["s_sred"], IC * NT)
            v.reduce_max(lmax2[:, :], smaxs[:, :IC * NT], AX.X).then_inc(sem["s_lf2"], 1)
            v.wait_ge(sem["s_b2"], 32)
            v.reduce_max(row2[:1, 128:129], row2[:1, :128], AX.X).then_inc(sem["s_g2"], 1)
            v.wait_ge(sem["s_b2"], 64)
            v.tensor_tensor(s2c[:, :], gmax2[:, :], sigd[:, :], ALU.mult).then_inc(sem["s_bv2"], 1)
            v.wait_ge(sem["s_bv2"], 1)
            v.tensor_scalar(s2c[:, :], s2c[:, :], 1.0 / QMAX, None, ALU.mult).then_inc(sem["s_bv2"], 1)
            v.wait_ge(sem["s_bv2"], 2)
            v.reciprocal(invs2[:, :], s2c[:, :])
            v.tensor_scalar(escd[:, :], rsd[:, :], s2c[:, :], 1.0 / QMAX, ALU.mult, ALU.mult)
            v.engine_nop().then_inc(sem["s_scl2"], 1)

            def _evacC(mt2, hc):
                ghc = 8 * mt2 + hc
                v.wait_ge(sem["s_om"], ghc + 1)
                v.wait_ge(axr[ghc % 8], 16 * (NT + ghc // 8 + 1))
                v.wait_ge(ospr[ghc % 2], 16 * (ghc // 2))
                v.scalar_tensor_tensor(outring[:, ghc % 2, :], o_ps[hc % 4][:, :],
                                       escd[:, hc:hc + 1], xaring[:, ghc % 8, :],
                                       ALU.mult, ALU.add).then_inc(sem["s_ov"], 1)

            for mt in range(NT):
                v.wait_ge(sem["s_scl2"], 1)
                v.wait_ge(sem["s_om"], 8 * max(0, mt - 1))
                for ic in range(IC):
                    gic = IC * mt + ic
                    v.wait_ge(clr[gic % 3], 16 * (gic // 3 + 1))
                    v.tensor_scalar(qfC[:, :], sinring[:, gic % 3, :],
                                    invs2[:, :], MAGIC, ALU.mult, ALU.add)
                    v.tensor_scalar(qfC[:, :], qfC[:, :],
                                    MAGIC + QMAX, MAGIC - QMAX - 1.0, ALU.min, ALU.max)
                    v.tensor_scalar(hqb[:, mt % 2, ic, :], qfC[:, :],
                                    -MAGIC, None, ALU.add).then_inc(sem["s_qc"], 1)
                    if mt >= 1 and ic >= 4 and ic % 2 == 0 and (ic - 4) // 2 < DC:
                        _evacC(mt - 1, (ic - 4) // 2)
                v.engine_nop().then_inc(sem["s_qt"], 1)
            for hc in range(DC):
                _evacC(NT - 1, hc)

    return nc


_NC_CACHE = None


def kernel(L0_attn_x, L0_attn_W_in, L0_attn_b_in, L0_attn_log_A, L0_attn_B_vec,
           L0_attn_C, L0_attn_D, L0_ffn_clip_x, L0_ffn_clip_down,
           L0_ffn_w_gate, L0_ffn_w_up, L0_ffn_w_down):
    global _NC_CACHE
    x = np.asarray(L0_attn_x, dtype=np.float32)
    B, T, Dm = x.shape
    assert (B, T, Dm) == (16, 4096, 1024)

    if _NC_CACHE is None:
        _NC_CACHE = build_nc()
    nc = _NC_CACHE

    shared = {
        "W_in": np.ascontiguousarray(L0_attn_W_in, dtype=np.float32),
        "C_": np.ascontiguousarray(L0_attn_C, dtype=np.float32),
        "b_in": np.ascontiguousarray(L0_attn_b_in, dtype=np.float32),
        "log_A": np.ascontiguousarray(L0_attn_log_A, dtype=np.float32),
        "B_vec": np.ascontiguousarray(L0_attn_B_vec, dtype=np.float32),
        "D_": np.ascontiguousarray(L0_attn_D, dtype=np.float32),
        "clip_x": np.ascontiguousarray(np.asarray(L0_ffn_clip_x, dtype=np.float32).reshape(1, 1)),
        "clip_down": np.ascontiguousarray(np.asarray(L0_ffn_clip_down, dtype=np.float32).reshape(1, 1)),
        "wg": np.ascontiguousarray(L0_ffn_w_gate, dtype=np.float32),
        "wu": np.ascontiguousarray(L0_ffn_w_up, dtype=np.float32),
        "wd": np.ascontiguousarray(L0_ffn_w_down, dtype=np.float32),
        "eye": np.eye(128, dtype=np.float32),
    }
    in_maps = []
    for c in range(NCORES):
        xc = np.ascontiguousarray(x[2 * c:2 * c + 2].reshape(NTOK, Dm).T)
        m = dict(shared)
        m["xT"] = xc
        in_maps.append(m)

    res = run_bass_kernel_spmd(nc, in_maps, core_ids=list(range(NCORES)))
    outs = []
    for c in range(NCORES):
        yc = np.asarray(res.results[c]["out"])
        outs.append(np.ascontiguousarray(yc.T).reshape(2, SEQ, Dm))
    return np.concatenate(outs, axis=0).astype(np.float32)
